# version 08 (final)
# speedup vs baseline: 1.6728x; 1.0025x over previous
import numpy as np
from contextlib import ExitStack

import concourse.bass as bass
import concourse.mybir as mybir
from concourse.bass_utils import run_bass_kernel_spmd

F32 = mybir.dt.float32
BF16 = mybir.dt.bfloat16
AF = mybir.ActivationFunctionType
ALU = mybir.AluOpType
AX = mybir.AxisListType

P = 128
NCORES = 8
B = 2
L = 8192
D = 1024
DIN = 2048
NS = 16
TOK = L // 4
CH = DIN // 4
EPS = 1e-5
BIG = 1.0e30

ENGS = ("sync", "gpsimd", "vector", "scalar", "tensor")


class Sem:
    def __init__(self, kb, name):
        self.h = kb.ctx.enter_context(kb.nc.semaphore(name))
        self.v = 0


class KB:

    def __init__(self, name="kb"):
        self.nc = bass.Bass("TRN2", target_bir_lowering=False, debug=False)
        self.ctx = ExitStack()
        self.ctx.__enter__()
        self.q = {e: [] for e in ENGS}
        self.name = name

    def sem(self, name):
        s = Sem(self, name)
        if not hasattr(self, "all_sems"):
            self.all_sems = []
        self.all_sems.append(s)
        return s

    def _uniq(self, name):
        if not hasattr(self, "_names"):
            self._names = {}
        n = self._names.get(name, 0)
        self._names[name] = n + 1
        return name if n == 0 else f"{name}__{n}"

    def sbuf(self, name, shape, dtype):
        ctx = getattr(self, "cur_ctx", None) or self.ctx
        return ctx.enter_context(self.nc.sbuf_tensor(self._uniq(name), shape, dtype))

    def psum(self, name, shape, dtype=F32):
        ctx = getattr(self, "cur_ctx", None) or self.ctx
        return ctx.enter_context(self.nc.psum_tensor(self._uniq(name), shape, dtype))

    def scope(self):
        from contextlib import contextmanager

        @contextmanager
        def _s():
            old = getattr(self, "cur_ctx", None)
            s = ExitStack()
            s.__enter__()
            self.cur_ctx = s
            try:
                yield
            finally:
                self.cur_ctx = old
                self.barrier_all()
                s.__exit__(None, None, None)
        return _s()

    def barrier_all(self):
        sems = list(getattr(self, "all_sems", []))
        for e in ENGS:
            vals = [(s.h, s.v) for s in sems if s.v > 0]

            def fn(eng, vals=vals):
                for h, v in vals:
                    eng.wait_ge(h, v)
            self.q[e].append(fn)

    def din(self, name, shape, dtype=F32):
        return self.nc.dram_tensor(name, shape, dtype, kind="ExternalInput").ap()

    def dout(self, name, shape, dtype=F32):
        return self.nc.dram_tensor(name, shape, dtype, kind="ExternalOutput").ap()

    def dint(self, name, shape, dtype=F32):
        return self.nc.dram_tensor(name, shape, dtype)

    def step(self, eng, emit, waits=(), inc=None, inc_n=1):
        wl = []
        for w in waits:
            if isinstance(w, tuple):
                wl.append((w[0].h, w[1]))
            else:
                wl.append((w.h, w.v))
        if inc is not None:
            inc.v += inc_n

        def fn(e, wl=wl, emit=emit, inc=inc, inc_n=inc_n):
            for h, v in wl:
                if v > 0:
                    e.wait_ge(h, v)
            last = emit(e)
            if inc is not None:
                assert last is not None, "step with inc must return an instruction"
                last.then_inc(inc.h, inc_n)

        self.q[eng].append(fn)

    def dma(self, eng, out, in_, waits=(), inc=None):
        self.step(eng, lambda e: e.dma_start(out=out, in_=in_),
                  waits=waits, inc=inc, inc_n=16)

    def barrier_wait(self, eng, sem):
        v = sem.v

        def fn(e, v=v, h=sem.h):
            if v > 0:
                e.wait_ge(h, v)
        self.q[eng].append(fn)

    def setup_consts(self, values, sem):
        vals = sorted(set(float(v) for v in values))
        pool = self.sbuf("constpool", [P, max(1, len(vals))], F32)
        for i, v in enumerate(vals):
            self.step("gpsimd", lambda e, i=i, v=v: e.memset(pool[:, i:i + 1], v),
                      inc=sem)
            self.nc.const_aps.register(F32, v, pool[:, i:i + 1])

    def build(self):
        block = self.ctx.enter_context(self.nc.Block())
        for e in ENGS:
            fns = self.q[e]
            if not fns:
                continue

            def body(eng, fns=fns):
                for f in fns:
                    f(eng)

            getattr(block, e)(body)
        self.ctx.close()
        return self.nc


def ceil_div(a, b):
    return (a + b - 1) // b



def emit_ln_tile(kb, x_t, nf, scr, cols, VD, SD, out_bf=None, out_f32=None,
                 lnw_bc=None, lnb_bc=None, waits=(), nrows=128):
    r = slice(0, nrows)
    w = list(waits)
    kb.step("vector", lambda e: e.reduce_sum(cols[r, 0:1], x_t[r, 0:nf], axis=AX.X),
            waits=w + [SD], inc=VD)
    kb.step("scalar", lambda e: e.activation(out=scr[r, 0:nf], in_=x_t[r, 0:nf],
                                             func=AF.Square,
                                             accum_out=cols[r, 1:2]),
            waits=w + [VD], inc=SD)

    kb.step("vector", lambda e: e.tensor_scalar(out=cols[r, 2:3], in0=cols[r, 0:1],
                                                scalar1=1.0 / nf, scalar2=None,
                                                op0=ALU.mult),
            waits=[VD, SD], inc=VD)
    kb.step("vector", lambda e: e.tensor_mul(cols[r, 3:4], cols[r, 2:3],
                                             cols[r, 2:3]),
            waits=[VD], inc=VD)
    kb.step("vector", lambda e: e.scalar_tensor_tensor(out=cols[r, 4:5],
                                                       in0=cols[r, 1:2],
                                                       scalar=1.0 / nf,
                                                       in1=cols[r, 3:4],
                                                       op0=ALU.mult,
                                                       op1=ALU.subtract),
            waits=[VD], inc=VD)
    kb.step("scalar", lambda e: e.activation(out=cols[r, 5:6], in_=cols[r, 4:5],
                                             func=AF.Sqrt, bias=EPS),
            waits=[VD], inc=SD)
    kb.step("vector", lambda e: e.reciprocal(cols[r, 6:7], cols[r, 5:6]),
            waits=[SD], inc=VD)
    kb.step("vector", lambda e: e.tensor_mul(cols[r, 7:8], cols[r, 2:3],
                                             cols[r, 6:7]),
            waits=[VD], inc=VD)
    kb.step("vector", lambda e: e.tensor_scalar(out=cols[r, 7:8], in0=cols[r, 7:8],
                                                scalar1=-1.0, scalar2=None,
                                                op0=ALU.mult),
            waits=[VD], inc=VD)
    kb.step("vector", lambda e: e.tensor_scalar(out=scr[r, 0:nf], in0=x_t[r, 0:nf],
                                                scalar1=cols[r, 6:7],
                                                scalar2=cols[r, 7:8],
                                                op0=ALU.mult, op1=ALU.add),
            waits=[VD], inc=SD)

    def finish(e):
        src = scr
        last = None
        if lnw_bc is not None:
            e.tensor_mul(scr[r, 0:nf], scr[r, 0:nf], lnw_bc[r, 0:nf])
            if out_f32 is not None:
                e.tensor_add(out_f32[r, 0:nf], scr[r, 0:nf], lnb_bc[r, 0:nf])
                src = out_f32
            else:
                return e.tensor_add(out_bf[r, 0:nf], scr[r, 0:nf], lnb_bc[r, 0:nf])
        else:
            if out_f32 is not None:
                e.tensor_copy(out=out_f32[r, 0:nf], in_=scr[r, 0:nf])
                src = out_f32
        if out_bf is not None:
            last = e.tensor_copy(out=out_bf[r, 0:nf], in_=src[r, 0:nf])
        elif src is out_f32:
            last = e.engine_nop()
        return last
    kb.step("vector", finish, waits=[SD], inc=VD)


def emit_softplus(kb, dst, src_fn, t1, t2, VD, SD, nrows=128, ncols=None,
                  waits=(), bias_ap=None):
    c = slice(0, ncols)
    r = slice(0, nrows)
    if bias_ap is not None:
        kb.step("vector", lambda e: e.tensor_scalar(out=t1[r, c], in0=src_fn(e),
                                                    scalar1=bias_ap, scalar2=None,
                                                    op0=ALU.add),
                waits=list(waits), inc=SD)
    else:
        kb.step("vector", lambda e: e.tensor_copy(out=t1[r, c], in_=src_fn(e)),
                waits=list(waits), inc=SD)
    kb.step("scalar", lambda e: e.activation(out=t2[r, c], in_=t1[r, c], func=AF.Abs),
            waits=[SD], inc=SD)
    kb.step("scalar", lambda e: e.activation(out=t2[r, c], in_=t2[r, c], func=AF.Exp,
                                             scale=-1.0),
            waits=[SD], inc=SD)
    kb.step("scalar", lambda e: e.activation(out=dst[r, c], in_=t2[r, c], func=AF.Ln,
                                             bias=1.0),
            waits=[SD], inc=SD)

    def fin(e):
        e.tensor_relu(t2[r, c], t1[r, c])
        return e.tensor_add(dst[r, c], dst[r, c], t2[r, c])
    kb.step("vector", fin, waits=[SD], inc=VD)



def build_launch_A():
    kb = KB("A")
    TH = TOK + 3
    CHUNKS = [(i * 512, min(512, TH - i * 512)) for i in range(5)]

    xh = kb.din("xh", [TH, D], F32)
    lnw_bc = kb.din("lnw_bc", [P, D], F32)
    lnb_bc = kb.din("lnb_bc", [P, D], F32)
    w_in = kb.din("w_in", [D, 4 * D], F32)
    convw = kb.din("convw", [P, 16, 4], F32)
    convb = kb.din("convb", [P, 16], F32)
    mask3 = kb.din("mask3", [P, 3], F32)
    xpw = kb.din("xpw", [DIN, 96], F32)
    dtw = kb.din("dtw", [64, DIN], F32)
    dtb = kb.din("dtb", [P, 16], F32)
    ident = kb.din("ident", [P, P], BF16)

    u_out = kb.dout("u_out", [16, P, TOK], BF16)
    dt_out = kb.dout("dt_out", [16, P, TOK], BF16)
    zs_out = kb.dout("zs_out", [16, P, TOK], BF16)
    bc_out = kb.dout("bc_out", [32, TOK], BF16)

    w_sb = kb.sbuf("w_sb", [P, 8, 4 * D], BF16)
    xpw_sb = kb.sbuf("xpw_sb", [P, 16, 96], BF16)
    dtw_sb = kb.sbuf("dtw_sb", [64, DIN], BF16)
    dtb_sb = kb.sbuf("dtb_sb", [P, 16], F32)
    convw_sb = kb.sbuf("convw_sb", [P, 16, 4], F32)
    convb_sb = kb.sbuf("convb_sb", [P, 16], F32)
    mask3_sb = kb.sbuf("mask3_sb", [P, 3], F32)
    lnw_sb = kb.sbuf("lnw_sb", [P, D], F32)
    lnb_sb = kb.sbuf("lnb_sb", [P, D], F32)
    id_sb = kb.sbuf("id_sb", [P, P], BF16)

    x_tm = kb.sbuf("x_tm", [P, 4, D], F32)
    scr = kb.sbuf("scr", [P, D], F32)
    cols = kb.sbuf("cols", [P, 8], F32)
    xn_tm = kb.sbuf("xn_tm", [P, 4, D], BF16)
    xn_fm = kb.sbuf("xn_fm", [P, 8, 512], BF16)
    xi_ring = kb.sbuf("xi_ring", [P, 2, 16, 516], BF16)
    zs_ch = kb.sbuf("zs_ch", [P, 16, 512], BF16)
    cacc = kb.sbuf("cacc", [P, 512], F32)
    u_ch = kb.sbuf("u_ch", [P, 16, 512], BF16)
    dbc_ch = kb.sbuf("dbc_ch", [96, 512], F32)
    dbc_bf = kb.sbuf("dbc_bf", [64, 512], BF16)
    dt_f32 = kb.sbuf("dt_f32", [P, 512], F32)
    dt_t1 = kb.sbuf("dt_t1", [P, 512], F32)
    dt_t2 = kb.sbuf("dt_t2", [P, 512], F32)
    dt_bf = kb.sbuf("dt_bf", [P, 512], BF16)

    ps_mm = [kb.psum(f"ps_mm{i}", [P, 512], F32) for i in range(2)]
    ps_tr = [kb.psum(f"ps_tr{i}", [P, 1024], BF16) for i in range(2)]
    ps_xp = kb.psum("ps_xp", [P, 512], F32)
    ps_dt = kb.psum("ps_dt", [P, 512], F32)

    LD = kb.sem("LD")
    VD = kb.sem("VD")
    SD = kb.sem("SD")
    TR = kb.sem("TR")
    MM = kb.sem("MM")
    EV = kb.sem("EV")
    ST = kb.sem("ST")

    kb.setup_consts([EPS, 1.0, -1.0], LD)
    for k in range(8):
        kb.dma("gpsimd", w_sb[:, k, :], w_in[k * P:(k + 1) * P, :], inc=LD)
    for a in range(16):
        kb.dma("gpsimd", xpw_sb[:, a, :], xpw[a * P:(a + 1) * P, :], inc=LD)
    kb.dma("gpsimd", dtw_sb[:, :], dtw[:, :], inc=LD)
    kb.dma("sync", dtb_sb[:, :], dtb[:, :], inc=LD)
    kb.dma("sync", convw_sb[:, :, :], convw[:, :, :], inc=LD)
    kb.dma("sync", convb_sb[:, :], convb[:, :], inc=LD)
    kb.dma("sync", mask3_sb[:, :], mask3[:, :], inc=LD)
    kb.dma("sync", lnw_sb[:, :], lnw_bc[:, :], inc=LD)
    kb.dma("sync", lnb_sb[:, :], lnb_bc[:, :], inc=LD)
    kb.dma("sync", id_sb[:, :], ident[:, :], inc=LD)
    static_ld = LD.v

    def conv_and_tail(ci):
        slot = ci % 2
        for a in range(16):
            def conv(e, a=a, slot=slot):
                e.tensor_scalar(out=cacc[:, :], in0=xi_ring[:, slot, a, 0:512],
                                scalar1=convw_sb[:, a, 0:1], scalar2=None,
                                op0=ALU.mult)
                last = None
                for kk in range(1, 4):
                    last = e.scalar_tensor_tensor(
                        out=cacc[:, :], in0=xi_ring[:, slot, a, kk:kk + 512],
                        scalar=convw_sb[:, a, kk:kk + 1],
                        in1=cacc[:, :], op0=ALU.mult, op1=ALU.add)
                return last
            kb.step("vector", conv, waits=[VD, SD], inc=VD)
            kb.step("scalar", lambda e, a=a: e.activation(
                out=u_ch[:, a, :], in_=cacc[:, :], func=AF.Silu,
                bias=convb_sb[:, a:a + 1]), waits=[VD, MM, ST], inc=SD)
            kb.dma("sync", u_out[a, :, ci * 512:(ci + 1) * 512], u_ch[:, a, :],
                   waits=[SD], inc=ST)

        def xproj(e):
            last = None
            for a in range(16):
                last = e.matmul(ps_xp[0:96, :], xpw_sb[:, a, :], u_ch[:, a, :],
                                start=(a == 0), stop=(a == 15))
            return last
        kb.step("tensor", xproj, waits=[SD, EV], inc=MM)
        kb.step("scalar", lambda e: e.activation(out=dbc_ch[0:96, :],
                                                 in_=ps_xp[0:96, :], func=AF.Copy),
                waits=[MM, ST], inc=EV)
        kb.step("vector", lambda e: e.tensor_copy(out=dbc_bf[0:64, :],
                                                  in_=dbc_ch[0:64, :]),
                waits=[EV, MM], inc=VD)
        kb.dma("gpsimd", bc_out[:, ci * 512:(ci + 1) * 512], dbc_ch[64:96, :],
               waits=[EV], inc=ST)
        for m in range(16):
            def dtmm(e, m=m):
                return e.matmul(ps_dt[:, :], dtw_sb[:, m * P:(m + 1) * P],
                                dbc_bf[0:64, :], start=True, stop=True)
            kb.step("tensor", dtmm, waits=[VD, SD], inc=MM)
            emit_softplus(kb, dt_f32, lambda e: ps_dt[:, :], dt_t1, dt_t2,
                          VD, SD, ncols=512, waits=[MM, VD],
                          bias_ap=dtb_sb[:, m:m + 1])
            kb.step("vector", lambda e: e.tensor_copy(out=dt_bf[:, :],
                                                      in_=dt_f32[:, :]),
                    waits=[VD, ST], inc=VD)
            kb.dma("sync", dt_out[m, :, ci * 512:(ci + 1) * 512], dt_bf[:, :],
                   waits=[VD], inc=ST)

    for i, (c0, ncols) in enumerate(CHUNKS):
        ntt = ceil_div(ncols, P)
        for tt in range(ntt):
            rows = min(P, ncols - tt * P)
            kb.dma("sync", x_tm[0:rows, tt, :],
                   xh[c0 + tt * P:c0 + tt * P + rows, :],
                   waits=[(LD, static_ld), SD, VD], inc=LD)
            emit_ln_tile(kb, x_tm[:, tt, :], D, scr, cols, VD, SD,
                         out_bf=xn_tm[:, tt, :], lnw_bc=lnw_sb, lnb_bc=lnb_sb,
                         waits=[LD], nrows=rows)
            for f in range(8):
                slot = f % 2

                def tr(e, f=f, tt=tt, rows=rows, slot=slot):
                    return e.transpose(ps_tr[slot][:, 0:rows],
                                       xn_tm[0:rows, tt, f * P:(f + 1) * P],
                                       id_sb[0:rows, 0:rows])
                kb.step("tensor", tr, waits=[VD, EV], inc=TR)

                def trev(e, f=f, tt=tt, rows=rows, slot=slot):
                    return e.tensor_copy(
                        out=xn_fm[:, f, tt * P:tt * P + rows],
                        in_=ps_tr[slot][:, 0:rows])
                kb.step("vector", trev, waits=[TR], inc=EV)
        for m in range(32):
            slot = m % 2

            def mm(e, m=m, ncols=ncols, slot=slot):
                last = None
                for k in range(8):
                    last = e.matmul(ps_mm[slot][:, 0:ncols],
                                    w_sb[:, k, m * P:(m + 1) * P],
                                    xn_fm[:, k, 0:ncols],
                                    start=(k == 0), stop=(k == 7))
                return last
            kb.step("tensor", mm, waits=[EV], inc=MM)
            if m < 16:
                def evxi(e, m=m, i=i, ncols=ncols, slot=slot):
                    return e.tensor_copy(out=xi_ring[:, i % 2, m, 0:ncols],
                                         in_=ps_mm[slot][:, 0:ncols])
                kb.step("vector", evxi, waits=[MM], inc=EV)
            else:
                def evz(e, m=m, ncols=ncols, slot=slot):
                    return e.activation(out=zs_ch[:, m - 16, 0:ncols],
                                        in_=ps_mm[slot][:, 0:ncols], func=AF.Silu)
                kb.step("scalar", evz, waits=[MM, ST], inc=EV)
                lo_x, hi_x = c0, c0 + ncols
                lo_r, hi_r = max(0, lo_x - 3), min(TOK, hi_x - 3)
                if hi_r > lo_r:
                    s0 = lo_r + 3 - lo_x
                    kb.dma("sync",
                           zs_out[m - 16, :, lo_r:hi_r],
                           zs_ch[:, m - 16, s0:s0 + (hi_r - lo_r)],
                           waits=[EV], inc=ST)
        if i == 0:
            def maskh(e):
                last = None
                for a in range(16):
                    last = e.tensor_mul(xi_ring[:, 0, a, 0:3],
                                        xi_ring[:, 0, a, 0:3], mask3_sb[:, :])
                return last
            kb.step("vector", maskh, waits=[EV, LD], inc=VD)
        if i >= 1:
            def patch(e, i=i):
                last = None
                for a in range(16):
                    last = e.tensor_copy(out=xi_ring[:, (i - 1) % 2, a, 512:515],
                                         in_=xi_ring[:, i % 2, a, 0:3])
                return last
            kb.step("vector", patch, waits=[EV], inc=VD)
            conv_and_tail(i - 1)

    kb.barrier_wait("sync", ST)
    return kb.build()


def prep_launch_A(inp, consts):
    x = inp["L0_attn_x"]
    lnw_bc = np.tile(np.asarray(inp["L0_attn_ln_w"])[None, :], (P, 1)).astype(np.float32)
    lnb_bc = np.tile(np.asarray(inp["L0_attn_ln_b"])[None, :], (P, 1)).astype(np.float32)
    convw = np.ascontiguousarray(
        np.asarray(inp["L0_attn_conv_w"]).reshape(16, P, 4).transpose(1, 0, 2))
    convb = np.ascontiguousarray(np.asarray(inp["L0_attn_conv_b"]).reshape(16, P).T)
    dtb = np.ascontiguousarray(np.asarray(inp["L0_attn_dt_proj_b"]).reshape(16, P).T)
    in_maps = []
    for k in range(NCORES):
        b, t0 = k // 4, (k % 4) * TOK
        if t0 == 0:
            xhs = np.concatenate([np.zeros((3, D), np.float32), x[b, 0:TOK]], axis=0)
            mask3 = np.zeros((P, 3), np.float32)
        else:
            xhs = x[b, t0 - 3:t0 + TOK]
            mask3 = np.ones((P, 3), np.float32)
        in_maps.append(dict(
            xh=np.ascontiguousarray(xhs),
            lnw_bc=lnw_bc, lnb_bc=lnb_bc,
            w_in=np.asarray(inp["L0_attn_in_proj_w"]),
            convw=convw, convb=convb, mask3=mask3,
            xpw=np.asarray(inp["L0_attn_x_proj_w"]),
            dtw=np.asarray(inp["L0_attn_dt_proj_w"]),
            dtb=dtb,
            ident=consts["ident_bf"],
        ))
    return in_maps



def build_launch_D():
    kb = KB("D")
    TH = TOK + 6
    CHUNKS = [(i * 512, min(512, TH - i * 512)) for i in range(5)]

    xh = kb.din("xh", [TH, D], F32)
    w_in = kb.din("w_in", [D, 4 * D], F32)
    convw_f = kb.din("convw_f", [P, 16, 4], F32)
    convb_f = kb.din("convb_f", [P, 16], F32)
    convw_b = kb.din("convw_b", [P, 16, 4], F32)
    convb_b = kb.din("convb_b", [P, 16], F32)
    xpw_f = kb.din("xpw_f", [DIN, 96], F32)
    xpw_b = kb.din("xpw_b", [DIN, 96], F32)
    dtw_f = kb.din("dtw_f", [64, DIN], F32)
    dtw_b = kb.din("dtw_b", [64, DIN], F32)
    dtb_f = kb.din("dtb_f", [P, 16], F32)
    dtb_b = kb.din("dtb_b", [P, 16], F32)
    ident = kb.din("ident", [P, P], BF16)

    u_f_out = kb.dout("u_f_out", [16, P, TOK], BF16)
    dt_f_out = kb.dout("dt_f_out", [16, P, TOK], BF16)
    u_b_out = kb.dout("u_b_out", [16, P, TOK], BF16)
    dt_b_out = kb.dout("dt_b_out", [16, P, TOK], BF16)
    zs_out = kb.dout("zs_out", [16, P, TOK], BF16)
    bc_f_out = kb.dout("bc_f_out", [32, TOK], BF16)
    bc_b_out = kb.dout("bc_b_out", [32, TOK], BF16)

    w_sb = kb.sbuf("w_sb", [P, 8, 4 * D], BF16)
    xpw_sb = kb.sbuf("xpw_sb", [P, 2, 16, 96], BF16)
    dtw_sb = kb.sbuf("dtw_sb", [64, 2, DIN], BF16)
    dtb_sb = kb.sbuf("dtb_sb", [P, 2, 16], F32)
    convw_sb = kb.sbuf("convw_sb", [P, 2, 16, 4], F32)
    convb_sb = kb.sbuf("convb_sb", [P, 2, 16], F32)
    id_sb = kb.sbuf("id_sb", [P, P], BF16)

    xn_tm = kb.sbuf("xn_tm", [P, 4, D], BF16)
    xn_fm = kb.sbuf("xn_fm", [P, 8, 512], BF16)
    xi_ring = kb.sbuf("xi_ring", [P, 2, 16, 518], BF16)
    zs_ch = kb.sbuf("zs_ch", [P, 16, 512], BF16)
    cacc = kb.sbuf("cacc", [P, 512], F32)
    u_ch = kb.sbuf("u_ch", [P, 16, 512], BF16)
    dbc_ch = kb.sbuf("dbc_ch", [96, 512], F32)
    dbc_bf = kb.sbuf("dbc_bf", [64, 512], BF16)
    dt_f32 = kb.sbuf("dt_f32", [P, 512], F32)
    dt_t1 = kb.sbuf("dt_t1", [P, 512], F32)
    dt_t2 = kb.sbuf("dt_t2", [P, 512], F32)
    dt_bf = kb.sbuf("dt_bf", [P, 512], BF16)

    ps_mm = [kb.psum(f"ps_mm{i}", [P, 512], F32) for i in range(2)]
    ps_tr = [kb.psum(f"ps_tr{i}", [P, 1024], BF16) for i in range(2)]
    ps_xp = kb.psum("ps_xp", [P, 512], F32)
    ps_dt = kb.psum("ps_dt", [P, 512], F32)

    LD = kb.sem("LD")
    VD = kb.sem("VD")
    SD = kb.sem("SD")
    TR = kb.sem("TR")
    MM = kb.sem("MM")
    EV = kb.sem("EV")
    ST = kb.sem("ST")

    kb.setup_consts([EPS, 1.0, -1.0], LD)
    for k in range(8):
        kb.dma("gpsimd", w_sb[:, k, :], w_in[k * P:(k + 1) * P, :], inc=LD)
    for d_i, (xpw, dtw, dtb, cw, cb) in enumerate(
            [(xpw_f, dtw_f, dtb_f, convw_f, convb_f),
             (xpw_b, dtw_b, dtb_b, convw_b, convb_b)]):
        for a in range(16):
            kb.dma("gpsimd", xpw_sb[:, d_i, a, :], xpw[a * P:(a + 1) * P, :], inc=LD)
        kb.dma("gpsimd", dtw_sb[:, d_i, :], dtw[:, :], inc=LD)
        kb.dma("sync", dtb_sb[:, d_i, :], dtb[:, :], inc=LD)
        kb.dma("sync", convw_sb[:, d_i, :, :], cw[:, :, :], inc=LD)
        kb.dma("sync", convb_sb[:, d_i, :], cb[:, :], inc=LD)
    kb.dma("sync", id_sb[:, :], ident[:, :], inc=LD)
    static_ld = LD.v

    def conv_and_tail(ci):
        slot = ci % 2
        for d_i, (shifts, u_out, dt_out, bc_out) in enumerate(
                [((0, 1, 2, 3), u_f_out, dt_f_out, bc_f_out),
                 ((6, 5, 4, 3), u_b_out, dt_b_out, bc_b_out)]):
            for a in range(16):
                def conv(e, a=a, slot=slot, d_i=d_i, shifts=shifts):
                    e.tensor_scalar(out=cacc[:, :],
                                    in0=xi_ring[:, slot, a, shifts[0]:shifts[0] + 512],
                                    scalar1=convw_sb[:, d_i, a, 0:1], scalar2=None,
                                    op0=ALU.mult)
                    last = None
                    for kk in range(1, 4):
                        last = e.scalar_tensor_tensor(
                            out=cacc[:, :],
                            in0=xi_ring[:, slot, a, shifts[kk]:shifts[kk] + 512],
                            scalar=convw_sb[:, d_i, a, kk:kk + 1],
                            in1=cacc[:, :], op0=ALU.mult, op1=ALU.add)
                    return last
                kb.step("vector", conv, waits=[VD, SD], inc=VD)
                kb.step("scalar", lambda e, a=a, d_i=d_i: e.activation(
                    out=u_ch[:, a, :], in_=cacc[:, :], func=AF.Silu,
                    bias=convb_sb[:, d_i, a:a + 1]), waits=[VD, MM, ST], inc=SD)
                kb.dma("sync", u_out[a, :, ci * 512:(ci + 1) * 512], u_ch[:, a, :],
                       waits=[SD], inc=ST)

            def xproj(e, d_i=d_i):
                last = None
                for a in range(16):
                    last = e.matmul(ps_xp[0:96, :], xpw_sb[:, d_i, a, :], u_ch[:, a, :],
                                    start=(a == 0), stop=(a == 15))
                return last
            kb.step("tensor", xproj, waits=[SD, EV], inc=MM)
            kb.step("scalar", lambda e: e.activation(out=dbc_ch[0:96, :],
                                                     in_=ps_xp[0:96, :], func=AF.Copy),
                    waits=[MM, ST], inc=EV)
            kb.step("vector", lambda e: e.tensor_copy(out=dbc_bf[0:64, :],
                                                      in_=dbc_ch[0:64, :]),
                    waits=[EV, MM], inc=VD)
            kb.dma("gpsimd", bc_out[:, ci * 512:(ci + 1) * 512], dbc_ch[64:96, :],
                   waits=[EV], inc=ST)
            for m in range(16):
                def dtmm(e, m=m, d_i=d_i):
                    return e.matmul(ps_dt[:, :], dtw_sb[:, d_i, m * P:(m + 1) * P],
                                    dbc_bf[0:64, :], start=True, stop=True)
                kb.step("tensor", dtmm, waits=[VD, SD], inc=MM)
                emit_softplus(kb, dt_f32, lambda e: ps_dt[:, :], dt_t1, dt_t2,
                              VD, SD, ncols=512, waits=[MM, VD],
                              bias_ap=dtb_sb[:, d_i, m:m + 1])
                kb.step("vector", lambda e: e.tensor_copy(out=dt_bf[:, :],
                                                          in_=dt_f32[:, :]),
                        waits=[VD, ST], inc=VD)
                kb.dma("sync", dt_out[m, :, ci * 512:(ci + 1) * 512], dt_bf[:, :],
                       waits=[VD], inc=ST)

    for i, (c0, ncols) in enumerate(CHUNKS):
        ntt = ceil_div(ncols, P)
        for tt in range(ntt):
            rows = min(P, ncols - tt * P)
            kb.dma("gpsimd", xn_tm[0:rows, tt, :],
                   xh[c0 + tt * P:c0 + tt * P + rows, :],
                   waits=[(LD, static_ld), TR], inc=LD)
            for f in range(8):
                slot = f % 2

                def tr(e, f=f, tt=tt, rows=rows, slot=slot):
                    return e.transpose(ps_tr[slot][:, 0:rows],
                                       xn_tm[0:rows, tt, f * P:(f + 1) * P],
                                       id_sb[0:rows, 0:rows])
                kb.step("tensor", tr, waits=[LD, EV], inc=TR)

                def trev(e, f=f, tt=tt, rows=rows, slot=slot):
                    return e.tensor_copy(
                        out=xn_fm[:, f, tt * P:tt * P + rows],
                        in_=ps_tr[slot][:, 0:rows])
                kb.step("vector", trev, waits=[TR], inc=EV)
        for m in range(32):
            slot = m % 2

            def mm(e, m=m, ncols=ncols, slot=slot):
                last = None
                for k in range(8):
                    last = e.matmul(ps_mm[slot][:, 0:ncols],
                                    w_sb[:, k, m * P:(m + 1) * P],
                                    xn_fm[:, k, 0:ncols],
                                    start=(k == 0), stop=(k == 7))
                return last
            kb.step("tensor", mm, waits=[EV], inc=MM)
            if m < 16:
                def evxi(e, m=m, i=i, ncols=ncols, slot=slot):
                    return e.tensor_copy(out=xi_ring[:, i % 2, m, 0:ncols],
                                         in_=ps_mm[slot][:, 0:ncols])
                kb.step("vector", evxi, waits=[MM], inc=EV)
            else:
                def evz(e, m=m, ncols=ncols, slot=slot):
                    return e.activation(out=zs_ch[:, m - 16, 0:ncols],
                                        in_=ps_mm[slot][:, 0:ncols], func=AF.Silu)
                kb.step("scalar", evz, waits=[MM, ST], inc=EV)
                lo_x, hi_x = c0, c0 + ncols
                lo_r, hi_r = max(0, lo_x - 3), min(TOK, hi_x - 3)
                if hi_r > lo_r:
                    s0 = lo_r + 3 - lo_x
                    kb.dma("sync",
                           zs_out[m - 16, :, lo_r:hi_r],
                           zs_ch[:, m - 16, s0:s0 + (hi_r - lo_r)],
                           waits=[EV], inc=ST)
        if i >= 1:
            def patch(e, i=i):
                last = None
                for a in range(16):
                    last = e.tensor_copy(out=xi_ring[:, (i - 1) % 2, a, 512:518],
                                         in_=xi_ring[:, i % 2, a, 0:6])
                return last
            kb.step("vector", patch, waits=[EV], inc=VD)
            conv_and_tail(i - 1)

    kb.barrier_wait("sync", ST)
    return kb.build()


def prep_launch_D(inp, consts, x2_full):
    w_inT = np.ascontiguousarray(np.asarray(inp["L1_attn_in_proj_w"]).T)
    xpw_fT = np.ascontiguousarray(np.asarray(inp["L1_attn_x_proj_w"]).T)
    xpw_bT = np.ascontiguousarray(np.asarray(inp["L1_attn_x_proj_wb"]).T)
    dtw_fT = np.ascontiguousarray(np.asarray(inp["L1_attn_dt_proj_w"]).T)
    dtw_bT = np.ascontiguousarray(np.asarray(inp["L1_attn_dt_proj_wb"]).T)

    def vec(v):
        return np.ascontiguousarray(np.asarray(v).reshape(16, P).T)

    def cwr(w):
        return np.ascontiguousarray(np.asarray(w).reshape(16, P, 4).transpose(1, 0, 2))

    in_maps = []
    for k in range(NCORES):
        b, t0 = k // 4, (k % 4) * TOK
        lo, hi = t0 - 3, t0 + TOK + 3
        pad_l, pad_r = max(0, -lo), max(0, hi - L)
        xhs = x2_full[b, max(0, lo):min(L, hi)]
        if pad_l or pad_r:
            xhs = np.concatenate([np.zeros((pad_l, D), np.float32), xhs,
                                  np.zeros((pad_r, D), np.float32)], axis=0)
        in_maps.append(dict(
            xh=np.ascontiguousarray(xhs),
            w_in=w_inT,
            convw_f=cwr(inp["L1_attn_conv_w"]), convb_f=vec(inp["L1_attn_conv_b"]),
            convw_b=cwr(inp["L1_attn_conv_wb"]), convb_b=vec(inp["L1_attn_conv_bb"]),
            xpw_f=xpw_fT, xpw_b=xpw_bT,
            dtw_f=dtw_fT, dtw_b=dtw_bT,
            dtb_f=vec(inp["L1_attn_dt_proj_b"]), dtb_b=vec(inp["L1_attn_dt_proj_bb"]),
            ident=consts["ident_bf"],
        ))
    return in_maps



def build_launch_scan(ndir=1):
    kb = KB("S")
    NSEG = 4
    SEG = L // NSEG

    ins = []
    for j in range(ndir):
        sfx = f"_d{j}"
        ins.append(dict(
            dt=kb.din("dt" + sfx, [4, P, L], BF16),
            u=kb.din("u" + sfx, [4, P, L], BF16),
            Bbc=kb.din("Bbc" + sfx, [NS, P, L], BF16),
            Cbc=kb.din("Cbc" + sfx, [NS, P, L], BF16),
            Acol=kb.din("Acol" + sfx, [P, 4, NS], F32),
            Dcol=kb.din("Dcol" + sfx, [P, 4], F32),
        ))
    zs_in = kb.din("zs", [4, P, L], BF16)
    y_out = kb.dout("y_out", [4, P, L], BF16)

    dt_sb = kb.sbuf("dt_sb", [P, L], BF16)
    u_sb = kb.sbuf("u_sb", [P, L], BF16)
    zs_sb = kb.sbuf("zs_sb", [P, SEG], BF16)
    dtu_sb = kb.sbuf("dtu_sb", [P, SEG], F32)
    Bbc_sb = kb.sbuf("Bbc_sb", [P, SEG], F32)
    Cbc_sb = kb.sbuf("Cbc_sb", [P, SEG], F32)
    a_sb = kb.sbuf("a_sb", [P, SEG], F32)
    b_sb = kb.sbuf("b_sb", [P, SEG], F32)
    h_sb = kb.sbuf("h_sb", [P, SEG], F32)
    ym_sb = kb.sbuf("ym_sb", [P, SEG], F32)
    y_sb = kb.sbuf("y_sb", [P, L], F32)
    ya_sb = kb.sbuf("ya_sb", [P, L], F32) if ndir > 1 else None
    u32_sb = kb.sbuf("u32_sb", [P, SEG], F32)
    ybf_sb = kb.sbuf("ybf_sb", [P, SEG], BF16)
    An_sb = kb.sbuf("An_sb", [P, ndir * 4 * NS], F32)
    Dc_sb = kb.sbuf("Dc_sb", [P, ndir * 4], F32)
    carry = kb.sbuf("carry", [P, NS], F32)

    LD = kb.sem("LD")
    VD = kb.sem("VD")
    SD = kb.sem("SD")
    ST = kb.sem("ST")

    kb.setup_consts([EPS, 1.0, -1.0], LD)

    for j in range(ndir):
        kb.dma("sync", An_sb[:, j * 64:(j + 1) * 64],
               ins[j]["Acol"].rearrange("p a s -> p (a s)"), inc=LD)
        kb.dma("sync", Dc_sb[:, j * 4:(j + 1) * 4], ins[j]["Dcol"][:, :], inc=LD)
    kb.step("scalar", lambda e: e.activation(out=An_sb[:, 0:ndir * 64],
                                             in_=An_sb[:, 0:ndir * 64],
                                             func=AF.Exp),
            waits=[LD], inc=SD)
    kb.step("vector", lambda e: e.tensor_scalar(out=An_sb[:, 0:ndir * 64],
                                                in0=An_sb[:, 0:ndir * 64],
                                                scalar1=-1.0, scalar2=None,
                                                op0=ALU.mult),
            waits=[SD], inc=VD)

    for a in range(4):
        for j in range(ndir):
            rev = (j == 1)
            kb.dma("gpsimd", dt_sb[:, :], ins[j]["dt"][a, :, :],
                   waits=[SD, VD], inc=LD)
            kb.dma("gpsimd", u_sb[:, :], ins[j]["u"][a, :, :],
                   waits=[SD, VD], inc=LD)
            segs = range(NSEG - 1, -1, -1) if rev else range(NSEG)
            for si, sg in enumerate(segs):
                gs = slice(sg * SEG, (sg + 1) * SEG)
                kb.step("vector", lambda e, gs=gs: e.tensor_mul(
                    dtu_sb[:, :], dt_sb[:, gs], u_sb[:, gs]),
                    waits=[LD, VD, SD], inc=VD)
                for s in range(NS):
                    kb.dma("gpsimd", Bbc_sb[:, :], ins[j]["Bbc"][s, :, gs],
                           waits=[VD], inc=LD)
                    kb.dma("gpsimd", Cbc_sb[:, :], ins[j]["Cbc"][s, :, gs],
                           waits=[VD], inc=LD)
                    kb.step("scalar", lambda e, a=a, j=j, s=s, gs=gs: e.activation(
                        out=a_sb[:, :], in_=dt_sb[:, gs], func=AF.Exp,
                        scale=An_sb[:, j * 64 + a * NS + s:j * 64 + a * NS + s + 1]),
                        waits=[VD, LD], inc=SD)
                    kb.step("vector", lambda e: e.tensor_mul(
                        b_sb[:, :], dtu_sb[:, :], Bbc_sb[:, :]),
                        waits=[LD, SD], inc=VD)

                    def scan(e, si=si, s=s, rev=rev):
                        init = 0.0 if si == 0 else carry[:, s:s + 1]
                        if rev:
                            return e.tensor_tensor_scan(
                                out=h_sb[:, ::-1], data0=a_sb[:, ::-1],
                                data1=b_sb[:, ::-1],
                                initial=init, op0=ALU.mult, op1=ALU.add)
                        return e.tensor_tensor_scan(
                            out=h_sb[:, :], data0=a_sb[:, :], data1=b_sb[:, :],
                            initial=init, op0=ALU.mult, op1=ALU.add)
                    kb.step("vector", scan, waits=[VD, SD], inc=VD)
                    if si < NSEG - 1:
                        cc = slice(0, 1) if rev else slice(SEG - 1, SEG)
                        kb.step("vector", lambda e, s=s, cc=cc: e.tensor_copy(
                            out=carry[:, s:s + 1], in_=h_sb[:, cc]),
                            waits=[VD], inc=VD)
                    kb.step("vector", lambda e: e.tensor_mul(
                        ym_sb[:, :], h_sb[:, :], Cbc_sb[:, :]),
                        waits=[VD], inc=VD)
                    if s == 0:
                        kb.step("vector", lambda e, gs=gs: e.tensor_copy(
                            out=y_sb[:, gs], in_=ym_sb[:, :]),
                            waits=[VD], inc=VD)
                    else:
                        kb.step("vector", lambda e, gs=gs: e.tensor_add(
                            y_sb[:, gs], y_sb[:, gs], ym_sb[:, :]),
                            waits=[VD], inc=VD)
                kb.step("vector", lambda e, gs=gs: e.tensor_copy(
                    out=u32_sb[:, :], in_=u_sb[:, gs]),
                    waits=[VD], inc=VD)
                kb.step("vector", lambda e, a=a, j=j, gs=gs: e.scalar_tensor_tensor(
                    out=y_sb[:, gs], in0=u32_sb[:, :],
                    scalar=Dc_sb[:, j * 4 + a:j * 4 + a + 1],
                    in1=y_sb[:, gs], op0=ALU.mult, op1=ALU.add),
                    waits=[VD], inc=VD)
                if ndir > 1:
                    if j == 0:
                        kb.step("vector", lambda e, gs=gs: e.tensor_copy(
                            out=ya_sb[:, gs], in_=y_sb[:, gs]),
                            waits=[VD], inc=VD)
                    else:
                        kb.step("vector", lambda e, gs=gs: e.tensor_add(
                            ya_sb[:, gs], ya_sb[:, gs], y_sb[:, gs]),
                            waits=[VD], inc=VD)
        ysrc = ya_sb if ndir > 1 else y_sb
        for sg in range(NSEG):
            gs = slice(sg * SEG, (sg + 1) * SEG)
            kb.dma("gpsimd", zs_sb[:, :], zs_in[a, :, gs], waits=[VD, ST], inc=LD)
            kb.step("vector", lambda e, gs=gs: e.tensor_mul(
                ybf_sb[:, :], ysrc[:, gs], zs_sb[:, :]),
                waits=[VD, LD, ST], inc=VD)
            kb.dma("sync", y_out[a, :, gs], ybf_sb[:, :], waits=[VD], inc=ST)

    kb.barrier_wait("sync", ST)
    return kb.build()


def prep_launch_scan(dirs, zs_full):
    in_maps = []
    for k in range(NCORES):
        b, c0 = k // 4, (k % 4) * CH
        m = {}
        for j, dd in enumerate(dirs):
            sfx = f"_d{j}"
            m["dt" + sfx] = np.ascontiguousarray(
                dd["dt"][b, c0:c0 + CH].reshape(4, P, L))
            m["u" + sfx] = np.ascontiguousarray(
                dd["u"][b, c0:c0 + CH].reshape(4, P, L))
            m["Bbc" + sfx] = np.ascontiguousarray(
                np.broadcast_to(dd["Bm"][b][:, None, :], (NS, P, L)))
            m["Cbc" + sfx] = np.ascontiguousarray(
                np.broadcast_to(dd["Cm"][b][:, None, :], (NS, P, L)))
            m["Acol" + sfx] = np.ascontiguousarray(
                np.asarray(dd["A_log"])[c0:c0 + CH].reshape(4, P, NS)
                .transpose(1, 0, 2)).astype(np.float32)
            m["Dcol" + sfx] = np.ascontiguousarray(
                np.asarray(dd["D"])[c0:c0 + CH].reshape(4, P).T).astype(np.float32)
        m["zs"] = np.ascontiguousarray(zs_full[b, c0:c0 + CH].reshape(4, P, L))
        in_maps.append(m)
    return in_maps


def assemble_scan_y(res):
    out = np.zeros((B, DIN, L), dtype=res[0]["y_out"].dtype)
    for k in range(NCORES):
        b, c0 = k // 4, (k % 4) * CH
        out[b, c0:c0 + CH] = res[k]["y_out"].reshape(CH, L)
    return out



def emit_gate_top2(kb, glog, gcol, gtmp, comb_slice, VD, SD, waits=()):
    NE = 8
    kb.step("vector", lambda e: e.reduce_max(gcol[:, 0:1], glog[:, 0:NE], axis=AX.X),
            waits=list(waits) + [VD, SD], inc=VD)
    kb.step("vector", lambda e: e.tensor_scalar(out=comb_slice, in0=glog[:, 0:NE],
                                                scalar1=gcol[:, 0:1], scalar2=None,
                                                op0=ALU.is_equal),
            waits=[VD], inc=VD)
    kb.step("vector", lambda e: e.tensor_scalar(out=gtmp[:, 0:NE], in0=comb_slice,
                                                scalar1=-BIG, scalar2=None,
                                                op0=ALU.mult),
            waits=[VD], inc=VD)
    kb.step("vector", lambda e: e.tensor_add(gtmp[:, 0:NE], gtmp[:, 0:NE],
                                             glog[:, 0:NE]),
            waits=[VD], inc=VD)
    kb.step("vector", lambda e: e.reduce_max(gcol[:, 1:2], gtmp[:, 0:NE], axis=AX.X),
            waits=[VD], inc=VD)
    kb.step("vector", lambda e: e.tensor_scalar(out=gtmp[:, 0:NE], in0=gtmp[:, 0:NE],
                                                scalar1=gcol[:, 1:2], scalar2=None,
                                                op0=ALU.is_equal),
            waits=[VD], inc=VD)
    kb.step("vector", lambda e: e.tensor_sub(gcol[:, 2:3], gcol[:, 1:2],
                                             gcol[:, 0:1]),
            waits=[VD], inc=VD)
    kb.step("scalar", lambda e: e.activation(out=gcol[:, 3:4], in_=gcol[:, 2:3],
                                             func=AF.Exp),
            waits=[VD], inc=SD)
    kb.step("vector", lambda e: e.tensor_scalar(out=gcol[:, 3:4], in0=gcol[:, 3:4],
                                                scalar1=1.0, scalar2=None,
                                                op0=ALU.add),
            waits=[SD], inc=VD)
    kb.step("vector", lambda e: e.reciprocal(gcol[:, 4:5], gcol[:, 3:4]),
            waits=[VD], inc=VD)
    kb.step("vector", lambda e: e.tensor_scalar(out=gcol[:, 5:6], in0=gcol[:, 4:5],
                                                scalar1=-1.0, scalar2=1.0,
                                                op0=ALU.mult, op1=ALU.add),
            waits=[VD], inc=VD)
    kb.step("vector", lambda e: e.tensor_scalar(out=comb_slice, in0=comb_slice,
                                                scalar1=gcol[:, 4:5], scalar2=None,
                                                op0=ALU.mult),
            waits=[VD], inc=VD)
    kb.step("vector", lambda e: e.scalar_tensor_tensor(out=comb_slice,
                                                       in0=gtmp[:, 0:NE],
                                                       scalar=gcol[:, 5:6],
                                                       in1=comb_slice,
                                                       op0=ALU.mult, op1=ALU.add),
            waits=[VD], inc=VD)


def build_launch_C():
    kb = KB("C")
    NE, HID = 8, 1536
    NTT = TOK // P
    NCHK = 4

    y_in = kb.din("y_in", [16, P, TOK], BF16)
    x_in = kb.din("x_in", [TOK, D], F32)
    wout = kb.din("wout", [DIN, D], F32)
    wr_in = kb.din("wr_in", [D, NE], F32)
    w1_in = kb.din("w1_in", [NE, D, HID], F32)
    w2_in = kb.din("w2_in", [NE, HID, D], F32)
    w3_in = kb.din("w3_in", [NE, D, HID], F32)
    ident = kb.din("ident", [P, P], BF16)

    x1_out = kb.dout("x1_out", [TOK, D], F32)
    x2_out = kb.dout("x2_out", [TOK, D], F32)

    wbuf = kb.sbuf("wbuf", [P, 16, D], BF16)
    w13 = kb.sbuf("w13", [P, 8, HID], BF16)
    wr_sb = kb.sbuf("wr_sb", [P, 8, NE], BF16)
    id_sb = kb.sbuf("id_sb", [P, P], BF16)
    y_blk = kb.sbuf("y_blk", [P, 16, P], BF16)
    x_t = kb.sbuf("x_t", [P, D], F32)
    res_t = kb.sbuf("res_t", [P, D], F32)
    scr = kb.sbuf("scr", [P, D], F32)
    cols = kb.sbuf("cols", [P, 8], F32)
    x1_sb = kb.sbuf("x1_sb", [P, NTT, D], F32)
    x1fm = kb.sbuf("x1fm", [P, 8, 512], BF16)
    x1bf = kb.sbuf("x1bf", [P, D], BF16)
    glog_cm = kb.sbuf("glog_cm", [8, 512], BF16)
    glog = kb.sbuf("glog", [P, 8], F32)
    gcol = kb.sbuf("gcol", [P, 8], F32)
    gtmp = kb.sbuf("gtmp", [P, 8], F32)
    comb = kb.sbuf("comb", [P, NTT, NE], F32)
    hsil = kb.sbuf("hsil", [P, 12, 512], BF16)
    gbuf = kb.sbuf("gbuf", [P, 12, 512], BF16)
    yff = kb.sbuf("yff", [P, 4, D], F32)

    ps_o = [kb.psum(f"ps_o{i}", [P, 512], F32) for i in range(2)]
    ps_tr = [kb.psum(f"ps_tr{i}", [P, 1024], BF16) for i in range(2)]
    ps_g = kb.psum("ps_g", [P, 512], F32)
    ps_h = kb.psum("ps_h", [P, 512], F32)
    ps_y = [kb.psum(f"ps_y{i}", [P, 512], F32) for i in range(2)]

    LD = kb.sem("LD")
    VD = kb.sem("VD")
    SD = kb.sem("SD")
    TR = kb.sem("TR")
    MM = kb.sem("MM")
    EV = kb.sem("EV")
    ST = kb.sem("ST")

    kb.setup_consts([EPS, 1.0, -1.0], LD)

    for a in range(16):
        kb.dma("gpsimd", wbuf[:, a, :], wout[a * P:(a + 1) * P, :], inc=LD)
    for k in range(8):
        kb.dma("gpsimd", wr_sb[:, k, :], wr_in[k * P:(k + 1) * P, :], inc=LD)
    kb.dma("sync", id_sb[:, :], ident[:, :], inc=LD)
    static_ld = LD.v

    for m in range(NTT):
        for a in range(16):
            kb.dma("sync", y_blk[:, a, :], y_in[a, :, m * P:(m + 1) * P],
                   waits=[(LD, static_ld), MM], inc=LD)
        kb.dma("sync", x_t[:, :], x_in[m * P:(m + 1) * P, :], waits=[VD], inc=LD)

        def mmop(e):
            last = None
            for a in range(16):
                for nn in range(2):
                    last = e.matmul(ps_o[nn][:, :], y_blk[:, a, :],
                                    wbuf[:, a, nn * 512:(nn + 1) * 512],
                                    start=(a == 0), stop=(a == 15))
            return last
        kb.step("tensor", mmop, waits=[LD, EV], inc=MM)
        for nn in range(2):
            kb.step("vector", lambda e, nn=nn: e.scalar_tensor_tensor(
                out=res_t[:, nn * 512:(nn + 1) * 512], in0=x_t[:, nn * 512:(nn + 1) * 512],
                scalar=2.0, in1=ps_o[nn][:, :], op0=ALU.mult, op1=ALU.add),
                waits=[MM, LD], inc=EV)
        emit_ln_tile(kb, res_t, D, scr, cols, VD, SD,
                     out_f32=x1_sb[:, m, :], out_bf=x1bf,
                     waits=[EV])
        kb.dma("sync", x1_out[m * P:(m + 1) * P, :], x1_sb[:, m, :],
               waits=[VD], inc=ST)

    for c in range(NCHK):
        cs = slice(c * 512, (c + 1) * 512)
        for tt in range(4):
            m = c * 4 + tt
            kb.step("vector", lambda e, m=m: e.tensor_copy(
                out=x1bf[:, :], in_=x1_sb[:, m, :]), waits=[VD, SD, TR], inc=VD)
            for f in range(8):
                slot = f % 2

                def tr(e, f=f, slot=slot):
                    return e.transpose(ps_tr[slot][:, :], x1bf[:, f * P:(f + 1) * P],
                                       id_sb[:, :])
                kb.step("tensor", tr, waits=[VD, EV], inc=TR)

                def trev(e, f=f, tt=tt, slot=slot):
                    return e.tensor_copy(out=x1fm[:, f, tt * P:(tt + 1) * P],
                                         in_=ps_tr[slot][:, 0:P])
                kb.step("vector", trev, waits=[TR], inc=EV)

        def gmm(e):
            last = None
            for k in range(8):
                last = e.matmul(ps_g[0:8, :], wr_sb[:, k, :], x1fm[:, k, :],
                                start=(k == 0), stop=(k == 7))
            return last
        kb.step("tensor", gmm, waits=[EV, VD], inc=MM)
        kb.step("vector", lambda e: e.tensor_copy(out=glog_cm[:, :], in_=ps_g[0:8, :]),
                waits=[MM], inc=EV)
        for tt in range(4):
            m = c * 4 + tt
            slot = tt % 2

            def gtr(e, tt=tt, slot=slot):
                return e.transpose(ps_tr[slot][:, 0:8],
                                   glog_cm[0:8, tt * P:(tt + 1) * P],
                                   id_sb[0:8, 0:8])
            kb.step("tensor", gtr, waits=[EV, VD], inc=TR)
            kb.step("vector", lambda e, slot=slot: e.tensor_copy(
                out=glog[:, 0:8], in_=ps_tr[slot][:, 0:8]),
                waits=[TR], inc=EV)
            emit_gate_top2(kb, glog, gcol, gtmp, comb[:, m, 0:NE], VD, SD,
                           waits=[EV])

        for e_i in range(NE):
            for k in range(8):
                kb.dma("gpsimd", w13[:, k, :], w1_in[e_i, k * P:(k + 1) * P, :],
                       waits=[MM], inc=LD)
            for k in range(12):
                kb.dma("gpsimd", wbuf[:, k, :], w2_in[e_i, k * P:(k + 1) * P, :],
                       waits=[MM], inc=LD)
            for hm in range(12):
                def hmm(e, hm=hm, cs=cs):
                    last = None
                    for k in range(8):
                        last = e.matmul(ps_h[:, :],
                                        w13[:, k, hm * P:(hm + 1) * P],
                                        x1fm[:, k, :],
                                        start=(k == 0), stop=(k == 7))
                    return last
                kb.step("tensor", hmm, waits=[LD, EV], inc=MM)
                kb.step("scalar", lambda e, hm=hm: e.activation(
                    out=hsil[:, hm, :], in_=ps_h[:, :], func=AF.Silu),
                    waits=[MM], inc=EV)
            for k in range(8):
                kb.dma("gpsimd", w13[:, k, :], w3_in[e_i, k * P:(k + 1) * P, :],
                       waits=[MM], inc=LD)
            for hm in range(12):
                def gmm2(e, hm=hm, cs=cs):
                    last = None
                    for k in range(8):
                        last = e.matmul(ps_h[:, :],
                                        w13[:, k, hm * P:(hm + 1) * P],
                                        x1fm[:, k, :],
                                        start=(k == 0), stop=(k == 7))
                    return last
                kb.step("tensor", gmm2, waits=[LD, EV], inc=MM)
                kb.step("scalar", lambda e, hm=hm: e.activation(
                    out=gbuf[:, hm, :], in_=ps_h[:, :], func=AF.Copy),
                    waits=[MM], inc=EV)
                kb.step("vector", lambda e, hm=hm: e.tensor_mul(
                    gbuf[:, hm, :], hsil[:, hm, :], gbuf[:, hm, :]),
                    waits=[EV], inc=VD)
            for tt in range(4):
                m = c * 4 + tt

                def ymm(e, tt=tt):
                    last = None
                    for hk in range(12):
                        for nn in range(2):
                            last = e.matmul(ps_y[nn][:, :],
                                            gbuf[:, hk, tt * P:(tt + 1) * P],
                                            wbuf[:, hk, nn * 512:(nn + 1) * 512],
                                            start=(hk == 0), stop=(hk == 11))
                    return last
                kb.step("tensor", ymm, waits=[VD, EV], inc=MM)
                for nn in range(2):
                    if e_i == 0:
                        kb.step("vector", lambda e, tt=tt, m=m, nn=nn, e_i=e_i:
                                e.tensor_scalar(
                                    out=yff[:, tt, nn * 512:(nn + 1) * 512],
                                    in0=ps_y[nn][:, :],
                                    scalar1=comb[:, m, e_i:e_i + 1],
                                    scalar2=None, op0=ALU.mult),
                                waits=[MM], inc=EV)
                    else:
                        kb.step("vector", lambda e, tt=tt, m=m, nn=nn, e_i=e_i:
                                e.scalar_tensor_tensor(
                                    out=yff[:, tt, nn * 512:(nn + 1) * 512],
                                    in0=ps_y[nn][:, :],
                                    scalar=comb[:, m, e_i:e_i + 1],
                                    in1=yff[:, tt, nn * 512:(nn + 1) * 512],
                                    op0=ALU.mult, op1=ALU.add),
                                waits=[MM], inc=EV)
        for tt in range(4):
            m = c * 4 + tt
            kb.step("vector", lambda e, tt=tt, m=m: e.tensor_add(
                res_t[:, :], x1_sb[:, m, :], yff[:, tt, :]),
                waits=[EV, VD, SD], inc=VD)
            emit_ln_tile(kb, res_t, D, scr, cols, VD, SD, out_f32=x_t,
                         waits=[VD, ST])
            kb.dma("sync", x2_out[m * P:(m + 1) * P, :], x_t[:, :],
                   waits=[VD, SD], inc=ST)

    kb.barrier_wait("sync", ST)
    return kb.build()


def prep_launch_C(inp, consts, y_full, x_full):
    in_maps = []
    for k in range(NCORES):
        b, t0 = k // 4, (k % 4) * TOK
        in_maps.append(dict(
            y_in=np.ascontiguousarray(y_full[b, :, t0:t0 + TOK].reshape(16, P, TOK)),
            x_in=np.ascontiguousarray(x_full[b, t0:t0 + TOK]),
            wout=np.asarray(inp["L0_attn_out_proj_w"]),
            wr_in=np.asarray(inp["L0_ffn_Wr"]),
            w1_in=np.asarray(inp["L0_ffn_W1"]),
            w2_in=np.asarray(inp["L0_ffn_W2"]),
            w3_in=np.asarray(inp["L0_ffn_W3"]),
            ident=consts["ident_bf"],
        ))
    return in_maps


def emit_colop(kb, fn, VD, waits=()):
    kb.step("vector", fn, waits=list(waits) + [VD], inc=VD)


def emit_gate_l1(kb, glog, gexp, gorig, gtmp, gp4, gcol, gbb_sb, comb_slice,
                 VD, SD, waits=()):
    E = 16
    w = list(waits)
    kb.step("vector", lambda e: e.reduce_max(gcol[:, 0:1], glog[:, 0:E], axis=AX.X),
            waits=w + [VD, SD], inc=VD)
    emit_colop(kb, lambda e: e.tensor_scalar(out=gcol[:, 1:2], in0=gcol[:, 0:1],
                                             scalar1=-1.0, scalar2=None,
                                             op0=ALU.mult), VD)
    kb.step("scalar", lambda e: e.activation(out=gexp[:, 0:E], in_=glog[:, 0:E],
                                             func=AF.Exp, bias=gcol[:, 1:2]),
            waits=[VD], inc=SD)
    kb.step("vector", lambda e: e.reduce_sum(gcol[:, 2:3], gexp[:, 0:E], axis=AX.X),
            waits=[SD], inc=VD)
    emit_colop(kb, lambda e: e.reciprocal(gcol[:, 3:4], gcol[:, 2:3]), VD)
    emit_colop(kb, lambda e: e.tensor_scalar(out=gorig[:, 0:E], in0=gexp[:, 0:E],
                                             scalar1=gcol[:, 3:4], scalar2=None,
                                             op0=ALU.mult), VD)
    emit_colop(kb, lambda e: e.tensor_add(gtmp[:, 0:E], gorig[:, 0:E],
                                          gbb_sb[:, 0:E]), VD)
    pairs = [(0, 1), (0, 2), (0, 3), (1, 2), (1, 3), (2, 3)]
    first = True
    for (j1, j2) in pairs:
        emit_colop(kb, lambda e, j1=j1, j2=j2: e.tensor_add(
            gp4[:, 8:12], gtmp[:, j1:E:4], gtmp[:, j2:E:4]), VD)
        if first:
            emit_colop(kb, lambda e: e.tensor_copy(out=gp4[:, 0:4],
                                                   in_=gp4[:, 8:12]), VD)
            first = False
        else:
            emit_colop(kb, lambda e: e.tensor_max(gp4[:, 0:4], gp4[:, 0:4],
                                                  gp4[:, 8:12]), VD)
    gpairs = [(0, 1), (0, 2), (0, 3), (1, 2), (1, 3), (2, 3)]
    first = True
    for (j1, j2) in gpairs:
        emit_colop(kb, lambda e, j1=j1, j2=j2: e.tensor_tensor(
            out=gcol[:, 5:6], in0=gp4[:, j1:j1 + 1], in1=gp4[:, j2:j2 + 1],
            op=ALU.min), VD)
        if first:
            emit_colop(kb, lambda e: e.tensor_copy(out=gcol[:, 4:5],
                                                   in_=gcol[:, 5:6]), VD)
            first = False
        else:
            emit_colop(kb, lambda e: e.tensor_max(gcol[:, 4:5], gcol[:, 4:5],
                                                  gcol[:, 5:6]), VD)
    emit_colop(kb, lambda e: e.tensor_scalar(out=gp4[:, 4:8], in0=gp4[:, 0:4],
                                             scalar1=gcol[:, 4:5], scalar2=None,
                                             op0=ALU.is_ge), VD)
    emit_colop(kb, lambda e: e.tensor_scalar(out=gp4[:, 4:8], in0=gp4[:, 4:8],
                                             scalar1=BIG, scalar2=-BIG,
                                             op0=ALU.mult, op1=ALU.add), VD)
    for j in range(4):
        emit_colop(kb, lambda e, j=j: e.tensor_copy(out=gexp[:, j:E:4],
                                                    in_=gp4[:, 4:8]), VD)
    emit_colop(kb, lambda e: e.tensor_add(gtmp[:, 0:E], gtmp[:, 0:E],
                                          gexp[:, 0:E]), VD)
    kb.step("vector", lambda e: e.reduce_max(gcol[:, 6:7], gtmp[:, 0:E], axis=AX.X),
            waits=[VD], inc=VD)
    emit_colop(kb, lambda e: e.tensor_scalar(out=gexp[:, 0:E], in0=gtmp[:, 0:E],
                                             scalar1=gcol[:, 6:7], scalar2=None,
                                             op0=ALU.is_equal), VD)
    emit_colop(kb, lambda e: e.scalar_tensor_tensor(out=gtmp[:, 0:E],
                                                    in0=gexp[:, 0:E], scalar=-BIG,
                                                    in1=gtmp[:, 0:E],
                                                    op0=ALU.mult, op1=ALU.add), VD)
    kb.step("vector", lambda e: e.reduce_max(gcol[:, 7:8], gtmp[:, 0:E], axis=AX.X),
            waits=[VD], inc=VD)
    emit_colop(kb, lambda e: e.tensor_scalar(out=gtmp[:, 0:E], in0=gtmp[:, 0:E],
                                             scalar1=gcol[:, 7:8], scalar2=None,
                                             op0=ALU.is_equal), VD)
    emit_colop(kb, lambda e: e.tensor_add(gexp[:, 0:E], gexp[:, 0:E],
                                          gtmp[:, 0:E]), VD)
    emit_colop(kb, lambda e: e.tensor_mul(comb_slice, gexp[:, 0:E],
                                          gorig[:, 0:E]), VD)


def build_launch_F():
    kb = KB("F")
    NE, INTER, NSH = 16, 512, 1024
    NTT = TOK // P
    NCHK = 4

    y_in = kb.din("y_in", [16, P, TOK], BF16)
    x_in = kb.din("x_in", [TOK, D], F32)
    wout = kb.din("wout", [DIN, D], F32)
    gw_in = kb.din("gw_in", [D, NE], F32)
    gb_in = kb.din("gb_in", [P, NE], F32)
    we1_in = kb.din("we1_in", [NE, D, INTER], F32)
    we2_in = kb.din("we2_in", [NE, INTER, D], F32)
    we3_in = kb.din("we3_in", [NE, D, INTER], F32)
    be1_in = kb.din("be1_in", [NE, P, 4], F32)
    be3_in = kb.din("be3_in", [NE, P, 4], F32)
    be2_in = kb.din("be2_in", [NE, P, D], F32)
    ws1_in = kb.din("ws1_in", [D, NSH], F32)
    ws2_in = kb.din("ws2_in", [NSH, D], F32)
    ws3_in = kb.din("ws3_in", [D, NSH], F32)
    bs1_in = kb.din("bs1_in", [P, 8], F32)
    bs3_in = kb.din("bs3_in", [P, 8], F32)
    bs2_in = kb.din("bs2_in", [P, D], F32)
    ident = kb.din("ident", [P, P], BF16)

    x3_out = kb.dout("x3_out", [TOK, D], F32)
    x4_out = kb.dout("x4_out", [TOK, D], F32)

    wbuf = kb.sbuf("wbuf", [P, 16, D], BF16)
    w13 = kb.sbuf("w13", [P, 8, NSH], BF16)
    gw_sb = kb.sbuf("gw_sb", [P, 8, NE], BF16)
    gbb_sb = kb.sbuf("gbb_sb", [P, NE], F32)
    id_sb = kb.sbuf("id_sb", [P, P], BF16)
    y_blk = kb.sbuf("y_blk", [P, 16, P], BF16)
    x_t = kb.sbuf("x_t", [P, D], F32)
    res_t = kb.sbuf("res_t", [P, D], F32)
    scr = kb.sbuf("scr", [P, D], F32)
    cols = kb.sbuf("cols", [P, 8], F32)
    x3_sb = kb.sbuf("x3_sb", [P, NTT, D], F32)
    x3fm = kb.sbuf("x3fm", [P, 8, 512], BF16)
    x3bf = kb.sbuf("x3bf", [P, D], BF16)
    glog_cm = kb.sbuf("glog_cm", [NE, 512], BF16)
    glog = kb.sbuf("glog", [P, NE], F32)
    gexp = kb.sbuf("gexp", [P, NE], F32)
    gorig = kb.sbuf("gorig", [P, NE], F32)
    gtmp = kb.sbuf("gtmp", [P, NE], F32)
    gp4 = kb.sbuf("gp4", [P, 12], F32)
    gcol = kb.sbuf("gcol", [P, 8], F32)
    comb = kb.sbuf("comb", [P, NTT, NE], F32)
    hsil = kb.sbuf("hsil", [P, 8, 512], BF16)
    gbuf = kb.sbuf("gbuf", [P, 8, 512], BF16)
    yff = kb.sbuf("yff", [P, 4, D], F32)
    be2_sb = kb.sbuf("be2_sb", [P, D], F32)
    bsml = kb.sbuf("bsml", [P, 2 * NE * 4 + 16 + D], F32)

    ps_o = [kb.psum(f"ps_o{i}", [P, 512], F32) for i in range(2)]
    ps_tr = [kb.psum(f"ps_tr{i}", [P, 1024], BF16) for i in range(2)]
    ps_g = kb.psum("ps_g", [P, 512], F32)
    ps_h = kb.psum("ps_h", [P, 512], F32)
    ps_y = [kb.psum(f"ps_y{i}", [P, 512], F32) for i in range(2)]

    LD = kb.sem("LD")
    VD = kb.sem("VD")
    SD = kb.sem("SD")
    TR = kb.sem("TR")
    MM = kb.sem("MM")
    EV = kb.sem("EV")
    ST = kb.sem("ST")

    kb.setup_consts([EPS, 1.0, -1.0], LD)

    for a in range(16):
        kb.dma("gpsimd", wbuf[:, a, :], wout[a * P:(a + 1) * P, :], inc=LD)
    for k in range(8):
        kb.dma("gpsimd", gw_sb[:, k, :], gw_in[k * P:(k + 1) * P, :], inc=LD)
    kb.dma("sync", gbb_sb[:, :], gb_in[:, :], inc=LD)
    kb.dma("sync", id_sb[:, :], ident[:, :], inc=LD)
    for e_i in range(NE):
        kb.dma("sync", bsml[:, e_i * 4:(e_i + 1) * 4], be1_in[e_i, :, :], inc=LD)
        kb.dma("sync", bsml[:, 64 + e_i * 4:64 + (e_i + 1) * 4], be3_in[e_i, :, :],
               inc=LD)
    kb.dma("sync", bsml[:, 128:136], bs1_in[:, :], inc=LD)
    kb.dma("sync", bsml[:, 136:144], bs3_in[:, :], inc=LD)
    kb.dma("sync", bsml[:, 144:144 + D], bs2_in[:, :], inc=LD)
    static_ld = LD.v

    for m in range(NTT):
        for a in range(16):
            kb.dma("sync", y_blk[:, a, :], y_in[a, :, m * P:(m + 1) * P],
                   waits=[(LD, static_ld), MM], inc=LD)
        kb.dma("sync", x_t[:, :], x_in[m * P:(m + 1) * P, :], waits=[VD], inc=LD)

        def mmop(e):
            last = None
            for a in range(16):
                for nn in range(2):
                    last = e.matmul(ps_o[nn][:, :], y_blk[:, a, :],
                                    wbuf[:, a, nn * 512:(nn + 1) * 512],
                                    start=(a == 0), stop=(a == 15))
            return last
        kb.step("tensor", mmop, waits=[LD, EV], inc=MM)
        for nn in range(2):
            kb.step("vector", lambda e, nn=nn: e.scalar_tensor_tensor(
                out=res_t[:, nn * 512:(nn + 1) * 512],
                in0=x_t[:, nn * 512:(nn + 1) * 512],
                scalar=1.0, in1=ps_o[nn][:, :], op0=ALU.mult, op1=ALU.add),
                waits=[MM, LD], inc=EV)
        emit_ln_tile(kb, res_t, D, scr, cols, VD, SD,
                     out_f32=x3_sb[:, m, :],
                     waits=[EV])
        kb.dma("sync", x3_out[m * P:(m + 1) * P, :], x3_sb[:, m, :],
               waits=[VD], inc=ST)

    for c in range(NCHK):
        for tt in range(4):
            m = c * 4 + tt
            kb.step("vector", lambda e, m=m: e.tensor_copy(
                out=x3bf[:, :], in_=x3_sb[:, m, :]), waits=[VD, SD, TR], inc=VD)
            for f in range(8):
                slot = f % 2

                def tr(e, f=f, slot=slot):
                    return e.transpose(ps_tr[slot][:, :], x3bf[:, f * P:(f + 1) * P],
                                       id_sb[:, :])
                kb.step("tensor", tr, waits=[VD, EV], inc=TR)

                def trev(e, f=f, tt=tt, slot=slot):
                    return e.tensor_copy(out=x3fm[:, f, tt * P:(tt + 1) * P],
                                         in_=ps_tr[slot][:, 0:P])
                kb.step("vector", trev, waits=[TR], inc=EV)

        def gmm(e):
            last = None
            for k in range(8):
                last = e.matmul(ps_g[:, :], gw_sb[:, k, :], x3fm[:, k, :],
                                start=(k == 0), stop=(k == 7))
            return last
        kb.step("tensor", gmm, waits=[EV, VD], inc=MM)
        kb.step("vector", lambda e: e.tensor_copy(out=glog_cm[:, :],
                                                  in_=ps_g[0:NE, :]),
                waits=[MM], inc=EV)
        for tt in range(4):
            m = c * 4 + tt
            slot = tt % 2

            def gtr(e, tt=tt, slot=slot):
                return e.transpose(ps_tr[slot][:, 0:NE],
                                   glog_cm[0:NE, tt * P:(tt + 1) * P],
                                   id_sb[0:NE, 0:NE])
            kb.step("tensor", gtr, waits=[EV, VD], inc=TR)
            kb.step("vector", lambda e, slot=slot: e.tensor_copy(
                out=glog[:, 0:NE], in_=ps_tr[slot][:, 0:NE]),
                waits=[TR], inc=EV)
            emit_gate_l1(kb, glog, gexp, gorig, gtmp, gp4, gcol, gbb_sb,
                         comb[:, m, 0:NE], VD, SD, waits=[EV])

        for e_i in range(NE):
            for k in range(8):
                kb.dma("gpsimd", w13[:, k, 0:INTER],
                       we1_in[e_i, k * P:(k + 1) * P, :], waits=[MM], inc=LD)
            for k in range(4):
                kb.dma("gpsimd", wbuf[:, k, :], we2_in[e_i, k * P:(k + 1) * P, :],
                       waits=[MM], inc=LD)
            kb.dma("sync", be2_sb[:, :], be2_in[e_i, :, :], waits=[VD], inc=LD)
            for hm in range(4):
                def hmm(e, hm=hm):
                    last = None
                    for k in range(8):
                        last = e.matmul(ps_h[:, :],
                                        w13[:, k, hm * P:(hm + 1) * P],
                                        x3fm[:, k, :],
                                        start=(k == 0), stop=(k == 7))
                    return last
                kb.step("tensor", hmm, waits=[LD, EV], inc=MM)
                kb.step("scalar", lambda e, hm=hm, e_i=e_i: e.activation(
                    out=hsil[:, hm, :], in_=ps_h[:, :], func=AF.Silu,
                    bias=bsml[:, e_i * 4 + hm:e_i * 4 + hm + 1]),
                    waits=[MM], inc=EV)
            for k in range(8):
                kb.dma("gpsimd", w13[:, k, 0:INTER],
                       we3_in[e_i, k * P:(k + 1) * P, :], waits=[MM], inc=LD)
            for hm in range(4):
                def gmm2(e, hm=hm):
                    last = None
                    for k in range(8):
                        last = e.matmul(ps_h[:, :],
                                        w13[:, k, hm * P:(hm + 1) * P],
                                        x3fm[:, k, :],
                                        start=(k == 0), stop=(k == 7))
                    return last
                kb.step("tensor", gmm2, waits=[LD, EV], inc=MM)
                kb.step("vector", lambda e, hm=hm, e_i=e_i: e.tensor_scalar(
                    out=gbuf[:, hm, :], in0=ps_h[:, :],
                    scalar1=bsml[:, 64 + e_i * 4 + hm:64 + e_i * 4 + hm + 1],
                    scalar2=None, op0=ALU.add),
                    waits=[MM, VD], inc=EV)
                kb.step("vector", lambda e, hm=hm: e.tensor_mul(
                    gbuf[:, hm, :], hsil[:, hm, :], gbuf[:, hm, :]),
                    waits=[EV], inc=VD)
            for tt in range(4):
                m = c * 4 + tt

                def ymm(e, tt=tt):
                    last = None
                    for hk in range(4):
                        for nn in range(2):
                            last = e.matmul(ps_y[nn][:, :],
                                            gbuf[:, hk, tt * P:(tt + 1) * P],
                                            wbuf[:, hk, nn * 512:(nn + 1) * 512],
                                            start=(hk == 0), stop=(hk == 3))
                    return last
                kb.step("tensor", ymm, waits=[VD, EV], inc=MM)
                for nn in range(2):
                    sl = slice(nn * 512, (nn + 1) * 512)
                    if e_i == 0:
                        kb.step("vector", lambda e, tt=tt, m=m, nn=nn, sl=sl, e_i=e_i:
                                e.tensor_scalar(
                                    out=yff[:, tt, sl], in0=ps_y[nn][:, :],
                                    scalar1=comb[:, m, e_i:e_i + 1],
                                    scalar2=None, op0=ALU.mult),
                                waits=[MM], inc=EV)
                    else:
                        kb.step("vector", lambda e, tt=tt, m=m, nn=nn, sl=sl, e_i=e_i:
                                e.scalar_tensor_tensor(
                                    out=yff[:, tt, sl], in0=ps_y[nn][:, :],
                                    scalar=comb[:, m, e_i:e_i + 1],
                                    in1=yff[:, tt, sl],
                                    op0=ALU.mult, op1=ALU.add),
                                waits=[MM], inc=EV)
                    kb.step("vector", lambda e, tt=tt, m=m, sl=sl, e_i=e_i:
                            e.scalar_tensor_tensor(
                                out=yff[:, tt, sl], in0=be2_sb[:, sl],
                                scalar=comb[:, m, e_i:e_i + 1],
                                in1=yff[:, tt, sl],
                                op0=ALU.mult, op1=ALU.add),
                            waits=[EV, LD], inc=VD)
        for k in range(8):
            kb.dma("gpsimd", w13[:, k, :], ws1_in[k * P:(k + 1) * P, :],
                   waits=[MM], inc=LD)
        for hm in range(8):
            def hmms(e, hm=hm):
                last = None
                for k in range(8):
                    last = e.matmul(ps_h[:, :], w13[:, k, hm * P:(hm + 1) * P],
                                    x3fm[:, k, :], start=(k == 0), stop=(k == 7))
                return last
            kb.step("tensor", hmms, waits=[LD, EV], inc=MM)
            kb.step("scalar", lambda e, hm=hm: e.activation(
                out=hsil[:, hm, :], in_=ps_h[:, :], func=AF.Silu,
                bias=bsml[:, 128 + hm:128 + hm + 1]),
                waits=[MM], inc=EV)
        for k in range(8):
            kb.dma("gpsimd", w13[:, k, :], ws3_in[k * P:(k + 1) * P, :],
                   waits=[MM], inc=LD)
        for hm in range(8):
            def gmms(e, hm=hm):
                last = None
                for k in range(8):
                    last = e.matmul(ps_h[:, :], w13[:, k, hm * P:(hm + 1) * P],
                                    x3fm[:, k, :], start=(k == 0), stop=(k == 7))
                return last
            kb.step("tensor", gmms, waits=[LD, EV], inc=MM)
            kb.step("vector", lambda e, hm=hm: e.tensor_scalar(
                out=gbuf[:, hm, :], in0=ps_h[:, :],
                scalar1=bsml[:, 136 + hm:136 + hm + 1], scalar2=None, op0=ALU.add),
                waits=[MM, VD], inc=EV)
            kb.step("vector", lambda e, hm=hm: e.tensor_mul(
                gbuf[:, hm, :], hsil[:, hm, :], gbuf[:, hm, :]),
                waits=[EV], inc=VD)
        for k in range(8):
            kb.dma("gpsimd", wbuf[:, k, :], ws2_in[k * P:(k + 1) * P, :],
                   waits=[MM], inc=LD)
        for tt in range(4):
            def ymms(e, tt=tt):
                last = None
                for hk in range(8):
                    for nn in range(2):
                        last = e.matmul(ps_y[nn][:, :],
                                        gbuf[:, hk, tt * P:(tt + 1) * P],
                                        wbuf[:, hk, nn * 512:(nn + 1) * 512],
                                        start=(hk == 0), stop=(hk == 7))
                return last
            kb.step("tensor", ymms, waits=[VD, EV, LD], inc=MM)
            for nn in range(2):
                sl = slice(nn * 512, (nn + 1) * 512)
                kb.step("vector", lambda e, tt=tt, sl=sl, nn=nn:
                        e.scalar_tensor_tensor(
                            out=yff[:, tt, sl], in0=ps_y[nn][:, :], scalar=1.0,
                            in1=yff[:, tt, sl], op0=ALU.mult, op1=ALU.add),
                        waits=[MM, VD], inc=EV)
            kb.step("vector", lambda e, tt=tt: e.tensor_add(
                yff[:, tt, :], yff[:, tt, :], bsml[:, 144:144 + D]),
                waits=[EV], inc=VD)
        for tt in range(4):
            m = c * 4 + tt
            kb.step("vector", lambda e, tt=tt, m=m: e.tensor_add(
                res_t[:, :], x3_sb[:, m, :], yff[:, tt, :]),
                waits=[EV, VD, SD], inc=VD)
            emit_ln_tile(kb, res_t, D, scr, cols, VD, SD, out_f32=x_t,
                         waits=[VD, ST])
            kb.dma("sync", x4_out[m * P:(m + 1) * P, :], x_t[:, :],
                   waits=[VD, SD], inc=ST)

    kb.barrier_wait("sync", ST)
    return kb.build()


def prep_launch_F(inp, consts, y_full, x2_full):
    woutT = np.ascontiguousarray(np.asarray(inp["L1_attn_out_proj_w"]).T)
    gwT = np.ascontiguousarray(np.asarray(inp["L1_ffn_gate_w"]).T)
    gb_bc = np.tile(np.asarray(inp["L1_ffn_gate_b"])[None, :], (P, 1)).astype(np.float32)
    be1 = np.ascontiguousarray(
        np.asarray(inp["L1_ffn_be1"]).reshape(16, 4, P).transpose(0, 2, 1))
    be3 = np.ascontiguousarray(
        np.asarray(inp["L1_ffn_be3"]).reshape(16, 4, P).transpose(0, 2, 1))
    be2 = np.ascontiguousarray(
        np.broadcast_to(np.asarray(inp["L1_ffn_be2"])[:, None, :], (16, P, D)))
    bs1 = np.ascontiguousarray(np.asarray(inp["L1_ffn_bs1"]).reshape(8, P).T)
    bs3 = np.ascontiguousarray(np.asarray(inp["L1_ffn_bs3"]).reshape(8, P).T)
    bs2 = np.tile(np.asarray(inp["L1_ffn_bs2"])[None, :], (P, 1)).astype(np.float32)
    in_maps = []
    for k in range(NCORES):
        b, t0 = k // 4, (k % 4) * TOK
        in_maps.append(dict(
            y_in=np.ascontiguousarray(y_full[b, :, t0:t0 + TOK].reshape(16, P, TOK)),
            x_in=np.ascontiguousarray(x2_full[b, t0:t0 + TOK]),
            wout=woutT, gw_in=gwT, gb_in=gb_bc,
            we1_in=np.asarray(inp["L1_ffn_We1"]),
            we2_in=np.asarray(inp["L1_ffn_We2"]),
            we3_in=np.asarray(inp["L1_ffn_We3"]),
            be1_in=be1, be3_in=be3, be2_in=be2,
            ws1_in=np.asarray(inp["L1_ffn_ws1"]),
            ws2_in=np.asarray(inp["L1_ffn_ws2"]),
            ws3_in=np.asarray(inp["L1_ffn_ws3"]),
            bs1_in=bs1, bs3_in=bs3, bs2_in=bs2,
            ident=consts["ident_bf"],
        ))
    return in_maps


def make_consts():
    import ml_dtypes
    return dict(ident_bf=np.eye(P, dtype=np.float32).astype(ml_dtypes.bfloat16))


def assemble_ch_major(res, key):
    out = np.zeros((B, DIN, L), dtype=res[0][key].dtype)
    for k in range(NCORES):
        b, t0 = k // 4, (k % 4) * TOK
        out[b, :, t0:t0 + TOK] = res[k][key].reshape(DIN, TOK)
    return out


def assemble_rows(res, key, width=D):
    out = np.zeros((B, L, width), dtype=res[0][key].dtype)
    for k in range(NCORES):
        b, t0 = k // 4, (k % 4) * TOK
        out[b, t0:t0 + TOK] = res[k][key]
    return out



TOKB = 1024


def build_fused(upto="x4"):
    import concourse.aot_env as aot_env
    kb = KB("FUSED")
    NSEG = 4
    SEG = L // NSEG

    xq = kb.din("xq", [2, TOKB + 3, D], F32)
    mask3 = kb.din("mask3", [P, 3], F32)
    hsel = kb.din("hsel", [96, 12], F32)
    ident = kb.din("ident", [P, P], BF16)
    lnw_bc = kb.din("lnw_bc", [P, D], F32)
    lnb_bc = kb.din("lnb_bc", [P, D], F32)
    acol0 = kb.din("acol0", [P, 2 * NS], F32)
    dcol0 = kb.din("dcol0", [P, 2], F32)
    acol1f = kb.din("acol1f", [P, 2 * NS], F32)
    dcol1f = kb.din("dcol1f", [P, 2], F32)
    acol1b = kb.din("acol1b", [P, 2 * NS], F32)
    dcol1b = kb.din("dcol1b", [P, 2], F32)

    w_in0 = kb.din("w_in0", [D, 4 * D], F32)
    convw0 = kb.din("convw0", [P, 16, 4], F32)
    convb0 = kb.din("convb0", [P, 16], F32)
    xpw0 = kb.din("xpw0", [DIN, 96], F32)
    dtw0 = kb.din("dtw0", [64, DIN], F32)
    dtb0 = kb.din("dtb0", [P, 16], F32)
    wout0 = kb.din("wout0", [DIN, D], F32)
    wr0 = kb.din("wr0", [D, 8], F32)
    w10 = kb.din("w10", [8, D, 1536], F32)
    w20 = kb.din("w20", [8, 1536, D], F32)
    w30 = kb.din("w30", [8, D, 1536], F32)

    w_in1 = kb.din("w_in1", [D, 4 * D], F32)
    convwf = kb.din("convwf", [P, 16, 4], F32)
    convbf = kb.din("convbf", [P, 16], F32)
    convwb = kb.din("convwb", [P, 16, 4], F32)
    convbb = kb.din("convbb", [P, 16], F32)
    xpwf = kb.din("xpwf", [DIN, 96], F32)
    xpwb = kb.din("xpwb", [DIN, 96], F32)
    dtwf = kb.din("dtwf", [64, DIN], F32)
    dtwb = kb.din("dtwb", [64, DIN], F32)
    dtbf = kb.din("dtbf", [P, 16], F32)
    dtbb = kb.din("dtbb", [P, 16], F32)
    wout1 = kb.din("wout1", [DIN, D], F32)
    gw1 = kb.din("gw1", [D, 16], F32)
    gb1 = kb.din("gb1", [P, 16], F32)
    we1 = kb.din("we1", [16, D, 512], F32)
    we2 = kb.din("we2", [16, 512, D], F32)
    we3 = kb.din("we3", [16, D, 512], F32)
    be1 = kb.din("be1", [16, P, 4], F32)
    be3 = kb.din("be3", [16, P, 4], F32)
    be2 = kb.din("be2", [16, P, D], F32)
    ws1 = kb.din("ws1", [D, 1024], F32)
    ws2 = kb.din("ws2", [1024, D], F32)
    ws3 = kb.din("ws3", [D, 1024], F32)
    bs1 = kb.din("bs1", [P, 8], F32)
    bs3 = kb.din("bs3", [P, 8], F32)
    bs2 = kb.din("bs2", [P, D], F32)

    x4_out = kb.dout("x4_out", [2 * TOKB, D], F32)
    dbg_out = kb.dout("dbg_out", [2 * TOKB, D], F32) if upto != "x4" else None

    def a2a_pair(name):
        return (kb.dint(name + "_i", [4096, TOKB], BF16),
                kb.dint(name + "_o", [4096, TOKB], BF16))

    u0_i, u0_o = a2a_pair("u0")
    dt0_i, dt0_o = a2a_pair("dt0")
    zs0_i, zs0_o = a2a_pair("zs0")
    y0_i, y0_o = a2a_pair("y0")
    uf_i, uf_o = a2a_pair("uf")
    dtf_i, dtf_o = a2a_pair("dtf")
    ub_i, ub_o = a2a_pair("ub")
    dtb_i, dtb_o = a2a_pair("dtb")
    zs1_i, zs1_o = a2a_pair("zs1")
    y1_i, y1_o = a2a_pair("y1")
    bc0_i = kb.dint("bc0_i", [64, TOKB], BF16)
    bc0_g = kb.dint("bc0_g", [512, TOKB], BF16)
    bcf_i = kb.dint("bcf_i", [64, TOKB], BF16)
    bcf_g = kb.dint("bcf_g", [512, TOKB], BF16)
    bcb_i = kb.dint("bcb_i", [64, TOKB], BF16)
    bcb_g = kb.dint("bcb_g", [512, TOKB], BF16)
    hal_i = kb.dint("hal_i", [12, D], F32)
    hal_g = kb.dint("hal_g", [96, D], F32)
    x2loc = kb.dint("x2loc", [2 * TOKB, D], F32)
    x2h = kb.dint("x2h", [2, TOKB + 6, D], F32)

    LD = kb.sem("LD")
    VD = kb.sem("VD")
    SD = kb.sem("SD")
    TR = kb.sem("TR")
    MM = kb.sem("MM")
    EV = kb.sem("EV")
    ST = kb.sem("ST")
    CC = kb.sem("CC")

    kb.setup_consts([EPS, 1.0, -1.0], LD)
    RG = [list(range(8))]

    def coll(kind, ins, outs, waits):
        for i_ap, o_ap in zip(ins, outs):
            def emit(e, kind=kind, i_ap=i_ap, o_ap=o_ap):
                return e.collective_compute(
                    kind, ALU.bypass, replica_groups=RG,
                    ins=[i_ap.ap().opt()], outs=[o_ap.ap().opt()])
            kb.step("gpsimd", emit, waits=waits, inc=CC, inc_n=1)
        kb.barrier_all()

    def sec_front(layer):
        nhal = 3 if layer == 0 else 6
        TH = TOKB + nhal
        CHUNKS = [(0, 512), (512, 512), (1024, nhal)]
        w_src = w_in0 if layer == 0 else w_in1

        w_sb = kb.sbuf("fw_sb", [P, 8, 4 * D], BF16)
        xpw_sb = kb.sbuf("fxpw", [P, 2, 16, 96], BF16)
        dtw_sb = kb.sbuf("fdtw", [64, 2, DIN], BF16)
        dtb_sb = kb.sbuf("fdtb", [P, 2, 16], F32)
        convw_sb = kb.sbuf("fconvw", [P, 2, 16, 4], F32)
        convb_sb = kb.sbuf("fconvb", [P, 2, 16], F32)
        id_sb = kb.sbuf("fid", [P, P], BF16)
        mask3_sb = kb.sbuf("fmask3", [P, 3], F32)
        lnw_sb = kb.sbuf("flnw", [P, D], F32)
        lnb_sb = kb.sbuf("flnb", [P, D], F32)
        x_tm = kb.sbuf("fx_tm", [P, 4, D], F32)
        scr = kb.sbuf("fscr", [P, D], F32)
        cols = kb.sbuf("fcols", [P, 8], F32)
        xn_tm = kb.sbuf("fxn_tm", [P, 4, D], BF16)
        xn_fm = kb.sbuf("fxn_fm", [P, 8, 512], BF16)
        xi_ring = kb.sbuf("fxi_ring", [P, 2, 16, 512 + 2 * nhal], BF16)
        zs_ch = kb.sbuf("fzs_ch", [P, 2, 512], BF16)
        cacc = kb.sbuf("fcacc", [P, 2, 512], F32)
        u_ch = kb.sbuf("fu_ch", [P, 2, 16, 512], BF16)
        dbc_ch = kb.sbuf("fdbc", [96, 512], F32)
        dbc_bf = kb.sbuf("fdbcbf", [64, 512], BF16)
        dt_f32 = kb.sbuf("fdtf32", [P, 512], F32)
        dt_t1 = kb.sbuf("fdt1", [P, 512], F32)
        dt_t2 = kb.sbuf("fdt2", [P, 512], F32)
        dt_bf = kb.sbuf("fdtbfo", [P, 512], BF16)

        ps_mm = [kb.psum(f"fpsmm{i}", [P, 512], F32) for i in range(2)]
        ps_tr = [kb.psum(f"fpstr{i}", [P, 1024], BF16) for i in range(2)]
        ps_xp = kb.psum("fpsxp", [P, 512], F32)
        ps_dt = kb.psum("fpsdt", [P, 512], F32)

        for k in range(8):
            kb.dma("gpsimd", w_sb[:, k, :], w_src[k * P:(k + 1) * P, :], inc=LD)
        if layer == 0:
            dirs = [(xpw0, dtw0, dtb0, convw0, convb0)]
        else:
            dirs = [(xpwf, dtwf, dtbf, convwf, convbf),
                    (xpwb, dtwb, dtbb, convwb, convbb)]
        for d_i, (xpw_, dtw_, dtb_, cw_, cb_) in enumerate(dirs):
            for a in range(16):
                kb.dma("gpsimd", xpw_sb[:, d_i, a, :], xpw_[a * P:(a + 1) * P, :],
                       inc=LD)
            kb.dma("gpsimd", dtw_sb[:, d_i, :], dtw_[:, :], inc=LD)
            kb.dma("sync", dtb_sb[:, d_i, :], dtb_[:, :], inc=LD)
            kb.dma("sync", convw_sb[:, d_i, :, :], cw_[:, :, :], inc=LD)
            kb.dma("sync", convb_sb[:, d_i, :], cb_[:, :], inc=LD)
        kb.dma("sync", id_sb[:, :], ident[:, :], inc=LD)
        if layer == 0:
            kb.dma("sync", mask3_sb[:, :], mask3[:, :], inc=LD)
            kb.dma("sync", lnw_sb[:, :], lnw_bc[:, :], inc=LD)
            kb.dma("sync", lnb_sb[:, :], lnb_bc[:, :], inc=LD)
        static_ld = LD.v

        if layer == 0:
            outs = [dict(u=u0_i, dt=dt0_i, bc=bc0_i,
                         shifts=(0, 1, 2, 3))]
            zs_dst = zs0_i
        else:
            outs = [dict(u=uf_i, dt=dtf_i, bc=bcf_i, shifts=(0, 1, 2, 3)),
                    dict(u=ub_i, dt=dtb_i, bc=bcb_i, shifts=(6, 5, 4, 3))]
            zs_dst = zs1_i

        zguard = [0, 0]

        def chrow(a, b):
            return (a // 2) * 512 + b * 256 + (a % 2) * 128

        uch_guard = [[(MM, 0)], [(MM, 0)]]
        cacc_guard = [[], []]
        dtbf_vd = [0]

        def conv_and_tail(b, ci):
            slot = ci % 2
            patch_vd = VD.v
            for d_i, oo in enumerate(outs):
                shifts = oo["shifts"]
                conv_vds = []
                for a in range(16):
                    def conv(e, a=a, slot=slot, d_i=d_i, shifts=shifts):
                        e.tensor_scalar(
                            out=cacc[:, d_i, :],
                            in0=xi_ring[:, slot, a, shifts[0]:shifts[0] + 512],
                            scalar1=convw_sb[:, d_i, a, 0:1], scalar2=None,
                            op0=ALU.mult)
                        last = None
                        for kk in range(1, 4):
                            last = e.scalar_tensor_tensor(
                                out=cacc[:, d_i, :],
                                in0=xi_ring[:, slot, a, shifts[kk]:shifts[kk] + 512],
                                scalar=convw_sb[:, d_i, a, kk:kk + 1],
                                in1=cacc[:, d_i, :], op0=ALU.mult, op1=ALU.add)
                        return last
                    kb.step("vector", conv,
                            waits=[(VD, patch_vd)] + cacc_guard[d_i], inc=VD)
                    cvd = VD.v
                    kb.step("scalar", lambda e, a=a, d_i=d_i: e.activation(
                        out=u_ch[:, d_i, a, :], in_=cacc[:, d_i, :], func=AF.Silu,
                        bias=convb_sb[:, d_i, a:a + 1]),
                        waits=[(VD, cvd)] + uch_guard[d_i] + [(ST, ST.v)], inc=SD)
                    cacc_guard[d_i] = [(SD, SD.v)]
                    r0 = chrow(a, b)
                    kb.dma("sync", oo["u"][r0:r0 + P, ci * 512:(ci + 1) * 512],
                           u_ch[:, d_i, a, :], waits=[SD], inc=ST)
                silu_sd = SD.v
                cacc_guard[d_i] = [(SD, silu_sd)]

                def xproj(e, d_i=d_i):
                    last = None
                    for a in range(16):
                        last = e.matmul(ps_xp[0:96, :], xpw_sb[:, d_i, a, :],
                                        u_ch[:, d_i, a, :],
                                        start=(a == 0), stop=(a == 15))
                    return last
                kb.step("tensor", xproj, waits=[(SD, silu_sd), EV], inc=MM)
                uch_guard[d_i] = [(MM, MM.v)]
                kb.step("scalar", lambda e: e.activation(
                    out=dbc_ch[0:96, :], in_=ps_xp[0:96, :], func=AF.Copy),
                    waits=[MM, ST], inc=EV)
                kb.step("vector", lambda e: e.tensor_copy(out=dbc_bf[0:64, :],
                                                          in_=dbc_ch[0:64, :]),
                        waits=[EV, MM], inc=VD)
                dbc_vd = VD.v
                kb.dma("gpsimd",
                       oo["bc"][b * 32:(b + 1) * 32, ci * 512:(ci + 1) * 512],
                       dbc_ch[64:96, :], waits=[EV], inc=ST)
                for m in range(16):
                    def dtmm(e, m=m, d_i=d_i):
                        return e.matmul(ps_dt[:, :],
                                        dtw_sb[:, d_i, m * P:(m + 1) * P],
                                        dbc_bf[0:64, :], start=True, stop=True)
                    kb.step("tensor", dtmm, waits=[(VD, dbc_vd), (SD, SD.v)],
                            inc=MM)
                    emit_softplus(kb, dt_f32, lambda e: ps_dt[:, :], dt_t1, dt_t2,
                                  VD, SD, ncols=512,
                                  waits=[MM, (VD, dtbf_vd[0])],
                                  bias_ap=dtb_sb[:, d_i, m:m + 1])
                    kb.step("vector", lambda e: e.tensor_copy(out=dt_bf[:, :],
                                                              in_=dt_f32[:, :]),
                            waits=[VD, ST], inc=VD)
                    dtbf_vd[0] = VD.v
                    r0 = chrow(m, b)
                    kb.dma("sync", oo["dt"][r0:r0 + P, ci * 512:(ci + 1) * 512],
                           dt_bf[:, :], waits=[VD], inc=ST)

        for b in range(2):
            for i, (c0, ncols) in enumerate(CHUNKS):
                ntt = ceil_div(ncols, P)
                for tt in range(ntt):
                    rows = min(P, ncols - tt * P)
                    if layer == 0:
                        kb.dma("sync", x_tm[0:rows, tt, :],
                               xq[b, c0 + tt * P:c0 + tt * P + rows, :],
                               waits=[(LD, static_ld), SD, VD], inc=LD)
                        emit_ln_tile(kb, x_tm[:, tt, :], D, scr, cols, VD, SD,
                                     out_bf=xn_tm[:, tt, :], lnw_bc=lnw_sb,
                                     lnb_bc=lnb_sb, waits=[LD], nrows=rows)
                    else:
                        kb.dma("gpsimd", xn_tm[0:rows, tt, :],
                               x2h[b, c0 + tt * P:c0 + tt * P + rows, :],
                               waits=[(LD, static_ld), TR, (CC, kb._hal_cc)],
                               inc=LD)
                    for f in range(8):
                        slot = f % 2

                        def tr(e, f=f, tt=tt, rows=rows, slot=slot):
                            return e.transpose(
                                ps_tr[slot][:, 0:rows],
                                xn_tm[0:rows, tt, f * P:(f + 1) * P],
                                id_sb[0:rows, 0:rows])
                        kb.step("tensor", tr, waits=[VD, LD, EV], inc=TR)

                        def trev(e, f=f, tt=tt, rows=rows, slot=slot):
                            return e.tensor_copy(
                                out=xn_fm[:, f, tt * P:tt * P + rows],
                                in_=ps_tr[slot][:, 0:rows])
                        kb.step("vector", trev, waits=[TR], inc=EV)
                for m in range(32):
                    slot = m % 2

                    def mm(e, m=m, ncols=ncols, slot=slot):
                        last = None
                        for k in range(8):
                            last = e.matmul(ps_mm[slot][:, 0:ncols],
                                            w_sb[:, k, m * P:(m + 1) * P],
                                            xn_fm[:, k, 0:ncols],
                                            start=(k == 0), stop=(k == 7))
                        return last
                    kb.step("tensor", mm, waits=[EV], inc=MM)
                    if m < 16:
                        def evxi(e, m=m, i=i, ncols=ncols, slot=slot):
                            return e.tensor_copy(out=xi_ring[:, i % 2, m, 0:ncols],
                                                 in_=ps_mm[slot][:, 0:ncols])
                        kb.step("vector", evxi, waits=[MM], inc=EV)
                    else:
                        zsl = (m - 16) % 2

                        def evz(e, m=m, ncols=ncols, slot=slot, zsl=zsl):
                            return e.activation(out=zs_ch[:, zsl, 0:ncols],
                                                in_=ps_mm[slot][:, 0:ncols],
                                                func=AF.Silu)
                        kb.step("scalar", evz,
                                waits=[MM, (ST, zguard[zsl])], inc=EV)
                        lo_x, hi_x = c0, c0 + ncols
                        lo_r, hi_r = max(0, lo_x - 3), min(TOKB, hi_x - 3)
                        if hi_r > lo_r:
                            s0 = lo_r + 3 - lo_x
                            r0 = chrow(m - 16, b)
                            kb.dma("sync",
                                   zs_dst[r0:r0 + P, lo_r:hi_r],
                                   zs_ch[:, zsl, s0:s0 + (hi_r - lo_r)],
                                   waits=[EV], inc=ST)
                            zguard[zsl] = ST.v
                if i == 0 and layer == 0:
                    def maskh(e):
                        last = None
                        for a in range(16):
                            last = e.tensor_mul(xi_ring[:, 0, a, 0:3],
                                                xi_ring[:, 0, a, 0:3],
                                                mask3_sb[:, :])
                        return last
                    kb.step("vector", maskh, waits=[EV, LD], inc=VD)
                if i >= 1:
                    def patch(e, i=i, nhal=nhal):
                        last = None
                        for a in range(16):
                            last = e.tensor_copy(
                                out=xi_ring[:, (i - 1) % 2, a, 512:512 + nhal],
                                in_=xi_ring[:, i % 2, a, 0:nhal])
                        return last
                    kb.step("vector", patch, waits=[EV], inc=VD)
                    conv_and_tail(b, i - 1)

    def sec_scan_new(layer):
        if layer == 0:
            dirs = [dict(u=u0_o, dt=dt0_o, bc=bc0_g, acol=acol0, dcol=dcol0,
                         rev=False)]
            zs_src, y_dst = zs0_o, y0_i
        else:
            dirs = [dict(u=uf_o, dt=dtf_o, bc=bcf_g, acol=acol1f, dcol=dcol1f,
                         rev=False),
                    dict(u=ub_o, dt=dtb_o, bc=bcb_g, acol=acol1b, dcol=dcol1b,
                         rev=True)]
            zs_src, y_dst = zs1_o, y1_i
        ndir = len(dirs)

        dt_sb = kb.sbuf("sdt", [P, 2, L], BF16)
        u_sb = kb.sbuf("su", [P, 2, L], BF16)
        zs_sb = kb.sbuf("szs", [P, SEG], BF16)
        dtu_sb = kb.sbuf("sdtu", [P, SEG], BF16)
        Bbc_sb = kb.sbuf("sB", [P, 2, SEG], BF16)
        Cbc_sb = kb.sbuf("sC", [P, 2, SEG], BF16)
        a_sb = kb.sbuf("sa", [P, 2, SEG], F32)
        b_sb = kb.sbuf("sb", [P, 2, SEG], F32)
        h_sb = kb.sbuf("sh", [P, SEG], BF16)
        ym_sb = kb.sbuf("sym", [P, SEG], BF16)
        du_sb = kb.sbuf("sdu", [P, SEG], BF16)
        y_sb = kb.sbuf("sy", [P, L], BF16)
        ya_sb = kb.sbuf("sya", [P, L], BF16) if ndir > 1 else None
        ybf_sb = kb.sbuf("sybf", [P, SEG], BF16)
        An_sb = kb.sbuf("sAn", [P, ndir * 2 * NS], F32)
        Dc_sb = kb.sbuf("sDc", [P, ndir * 2], F32)
        carry = kb.sbuf("scarry", [P, NS], F32)

        for j in range(ndir):
            kb.dma("sync", An_sb[:, j * 32:(j + 1) * 32],
                   dirs[j]["acol"][:, :], inc=LD)
            kb.dma("sync", Dc_sb[:, j * 2:(j + 1) * 2], dirs[j]["dcol"][:, :],
                   inc=LD)
        kb.step("scalar", lambda e, n=ndir: e.activation(
            out=An_sb[:, 0:n * 32], in_=An_sb[:, 0:n * 32], func=AF.Exp),
            waits=[LD], inc=SD)
        kb.step("vector", lambda e, n=ndir: e.tensor_scalar(
            out=An_sb[:, 0:n * 32], in0=An_sb[:, 0:n * 32],
            scalar1=-1.0, scalar2=None, op0=ALU.mult),
            waits=[SD], inc=VD)
        an_vd = VD.v

        GD = kb.sem("scanGD" + str(layer))

        SUBS = [(b, c128, j) for b in range(2) for c128 in range(2)
                for j in range(ndir)]
        slot_guard = [[], []]
        sub_ld = {}

        def load_dtu(q):
            if q >= len(SUBS):
                return
            b, c128, j = SUBS[q]
            sl = q % 2
            sub_r = b * 256 + c128 * 128
            dd = dirs[j]
            for w in range(8):
                kb.dma("sync", dt_sb[:, sl, w * TOKB:(w + 1) * TOKB],
                       dd["dt"][w * 512 + sub_r:w * 512 + sub_r + P, :],
                       waits=slot_guard[sl], inc=LD)
                kb.dma("sync", u_sb[:, sl, w * TOKB:(w + 1) * TOKB],
                       dd["u"][w * 512 + sub_r:w * 512 + sub_r + P, :],
                       waits=slot_guard[sl], inc=LD)
            sub_ld[q] = LD.v

        prev_scan = {}
        prev_ym = {}
        prev_gd = {}
        load_dtu(0)
        for q, (b, c128, j) in enumerate(SUBS):
            sl = q % 2
            if True:
                sub_r = b * 256 + c128 * 128
                if True:
                    dd = dirs[j]
                    rev = dd["rev"]
                    load_dtu(q + 1)
                    dtu_ld = sub_ld[q]
                    segs = range(NSEG - 1, -1, -1) if rev else range(NSEG)
                    bc_ld = {}
                    b_gd = {}
                    scan_vd = {}
                    ym_vd = {}
                    carry_vd = {}

                    def load_bc(s, sg):
                        slot = s % 2
                        for half in range(2):
                            w = sg * 2 + half
                            row = w * 64 + b * 32
                            hs = slice(half * TOKB, (half + 1) * TOKB)
                            if s - 2 in b_gd:
                                wts = [(GD, b_gd[s - 2]), (VD, ym_vd[s - 2])]
                            elif s - 2 + NS in prev_gd:
                                wts = [(GD, prev_gd[s - 2 + NS]),
                                       (VD, prev_ym[s - 2 + NS])]
                            else:
                                wts = [(GD, GD.v), (VD, VD.v)]
                            kb.dma("sync", Bbc_sb[:, slot, hs],
                                   dd["bc"][row + s:row + s + 1, :]
                                   .broadcast_to((P, TOKB)),
                                   waits=wts, inc=LD)
                            kb.dma("sync", Cbc_sb[:, slot, hs],
                                   dd["bc"][row + 16 + s:row + 16 + s + 1, :]
                                   .broadcast_to((P, TOKB)),
                                   waits=wts, inc=LD)
                        bc_ld[s] = LD.v

                    for si, sg in enumerate(segs):
                        gs = slice(sg * SEG, (sg + 1) * SEG)
                        kb.step("vector", lambda e, gs=gs, sl=sl: e.tensor_mul(
                            dtu_sb[:, :], dt_sb[:, sl, gs], u_sb[:, sl, gs]),
                            waits=[(LD, dtu_ld), VD, GD], inc=VD)
                        dtu_vd = VD.v
                        prev_scan = dict(scan_vd)
                        prev_ym = dict(ym_vd)
                        prev_gd = dict(b_gd)
                        bc_ld.clear(); b_gd.clear(); scan_vd.clear(); ym_vd.clear()
                        load_bc(0, sg)
                        for s in range(NS):
                            slot = s % 2
                            if s + 1 < NS:
                                load_bc(s + 1, sg)
                            aw = [(LD, dtu_ld), (VD, an_vd)]
                            if s - 2 in scan_vd:
                                aw.append((VD, scan_vd[s - 2]))
                            elif s - 2 + NS in prev_scan:
                                aw.append((VD, prev_scan[s - 2 + NS]))
                            kb.step("scalar",
                                    lambda e, j=j, c128=c128, s=s, gs=gs, slot=slot,
                                    sl=sl:
                                    e.activation(
                                        out=a_sb[:, slot, :], in_=dt_sb[:, sl, gs],
                                        func=AF.Exp,
                                        scale=An_sb[:, j * 32 + c128 * NS + s:
                                                    j * 32 + c128 * NS + s + 1]),
                                    waits=aw, inc=SD)
                            a_sd = SD.v
                            bw = [(LD, bc_ld[s]), (VD, dtu_vd)]
                            if s - 2 in scan_vd:
                                bw.append((VD, scan_vd[s - 2]))
                            elif s - 2 + NS in prev_scan:
                                bw.append((VD, prev_scan[s - 2 + NS]))
                            kb.step("gpsimd", lambda e, slot=slot: e.tensor_mul(
                                b_sb[:, slot, :], dtu_sb[:, :], Bbc_sb[:, slot, :]),
                                waits=bw, inc=GD)
                            b_gd[s] = GD.v

                            def scan(e, si=si, s=s, rev=rev, slot=slot):
                                init = 0.0 if si == 0 else carry[:, s:s + 1]
                                if rev:
                                    return e.tensor_tensor_scan(
                                        out=h_sb[:, ::-1],
                                        data0=a_sb[:, slot, ::-1],
                                        data1=b_sb[:, slot, ::-1],
                                        initial=init, op0=ALU.mult, op1=ALU.add)
                                return e.tensor_tensor_scan(
                                    out=h_sb[:, :], data0=a_sb[:, slot, :],
                                    data1=b_sb[:, slot, :],
                                    initial=init, op0=ALU.mult, op1=ALU.add)
                            sw = [(SD, a_sd), (GD, b_gd[s])]
                            if si > 0 and s in carry_vd:
                                sw.append((VD, carry_vd[s]))
                            kb.step("vector", scan, waits=sw, inc=VD)
                            scan_vd[s] = VD.v

                            def tail(e, si=si, s=s, rev=rev, slot=slot, gs=gs):
                                if si < NSEG - 1:
                                    cc = slice(0, 1) if rev else slice(SEG - 1, SEG)
                                    e.tensor_copy(out=carry[:, s:s + 1],
                                                  in_=h_sb[:, cc])
                                e.tensor_mul(ym_sb[:, :], h_sb[:, :],
                                             Cbc_sb[:, slot, :])
                                if s == 0:
                                    return e.tensor_copy(out=y_sb[:, gs],
                                                         in_=ym_sb[:, :])
                                return e.tensor_add(y_sb[:, gs], y_sb[:, gs],
                                                    ym_sb[:, :])
                            kb.step("vector", tail, waits=[], inc=VD)
                            carry_vd[s] = VD.v
                            ym_vd[s] = VD.v
                        def gate(e, j=j, c128=c128, gs=gs, sl=sl, ndir=ndir):
                            e.tensor_scalar(
                                out=du_sb[:, :], in0=u_sb[:, sl, gs],
                                scalar1=Dc_sb[:, j * 2 + c128:j * 2 + c128 + 1],
                                scalar2=None, op0=ALU.mult)
                            last = e.tensor_add(y_sb[:, gs], y_sb[:, gs],
                                                du_sb[:, :])
                            if ndir > 1:
                                if j == 0:
                                    last = e.tensor_copy(out=ya_sb[:, gs],
                                                         in_=y_sb[:, gs])
                                else:
                                    last = e.tensor_add(ya_sb[:, gs], ya_sb[:, gs],
                                                        y_sb[:, gs])
                            return last
                        kb.step("vector", gate, waits=[VD], inc=VD)
            slot_guard[sl] = [(SD, SD.v), (VD, VD.v)]
            if j == ndir - 1:
                ysrc = ya_sb if ndir > 1 else y_sb
                for sg in range(NSEG):
                    gs = slice(sg * SEG, (sg + 1) * SEG)
                    for half in range(2):
                        w = sg * 2 + half
                        hs = slice(half * TOKB, (half + 1) * TOKB)
                        kb.dma("sync", zs_sb[:, hs],
                               zs_src[w * 512 + sub_r:w * 512 + sub_r + P, :],
                               waits=[VD, ST], inc=LD)
                    kb.step("vector", lambda e, gs=gs: e.tensor_mul(
                        ybf_sb[:, :], ysrc[:, gs], zs_sb[:, :]),
                        waits=[VD, LD, ST], inc=VD)
                    for half in range(2):
                        w = sg * 2 + half
                        hs = slice(half * TOKB, (half + 1) * TOKB)
                        kb.dma("sync",
                               y_dst[w * 512 + sub_r:w * 512 + sub_r + P, :],
                               ybf_sb[:, hs], waits=[VD], inc=ST)

    def sec_scan(layer):
        if layer == 0:
            dirs = [dict(u=u0_o, dt=dt0_o, bc=bc0_g, acol=acol0, dcol=dcol0,
                         rev=False)]
            zs_src, y_dst = zs0_o, y0_i
        else:
            dirs = [dict(u=uf_o, dt=dtf_o, bc=bcf_g, acol=acol1f, dcol=dcol1f,
                         rev=False),
                    dict(u=ub_o, dt=dtb_o, bc=bcb_g, acol=acol1b, dcol=dcol1b,
                         rev=True)]
            zs_src, y_dst = zs1_o, y1_i
        ndir = len(dirs)

        dt_sb = kb.sbuf("sdt", [P, L], BF16)
        u_sb = kb.sbuf("su", [P, L], BF16)
        zs_sb = kb.sbuf("szs", [P, SEG], BF16)
        dtu_sb = kb.sbuf("sdtu", [P, SEG], BF16)
        Bbc_sb = kb.sbuf("sB", [P, 2, SEG], BF16)
        Cbc_sb = kb.sbuf("sC", [P, 2, SEG], BF16)
        a_sb = kb.sbuf("sa", [P, 2, SEG], F32)
        b_sb = kb.sbuf("sb", [P, 2, SEG], F32)
        h_sb = kb.sbuf("sh", [P, SEG], BF16)
        ym_sb = kb.sbuf("sym", [P, SEG], BF16)
        du_sb = kb.sbuf("sdu", [P, SEG], BF16)
        y_sb = kb.sbuf("sy", [P, L], BF16)
        ya_sb = kb.sbuf("sya", [P, L], BF16) if ndir > 1 else None
        ybf_sb = kb.sbuf("sybf", [P, SEG], BF16)
        An_sb = kb.sbuf("sAn", [P, ndir * 2 * NS], F32)
        Dc_sb = kb.sbuf("sDc", [P, ndir * 2], F32)
        carry = kb.sbuf("scarry", [P, NS], F32)

        for j in range(ndir):
            kb.dma("sync", An_sb[:, j * 32:(j + 1) * 32],
                   dirs[j]["acol"][:, :], inc=LD)
            kb.dma("sync", Dc_sb[:, j * 2:(j + 1) * 2], dirs[j]["dcol"][:, :],
                   inc=LD)
        kb.step("scalar", lambda e, n=ndir: e.activation(
            out=An_sb[:, 0:n * 32], in_=An_sb[:, 0:n * 32], func=AF.Exp),
            waits=[LD], inc=SD)
        kb.step("vector", lambda e, n=ndir: e.tensor_scalar(
            out=An_sb[:, 0:n * 32], in0=An_sb[:, 0:n * 32],
            scalar1=-1.0, scalar2=None, op0=ALU.mult),
            waits=[SD], inc=VD)
        an_vd = VD.v

        GD = kb.sem("scanGD" + str(layer))

        for b in range(2):
            for c128 in range(2):
                sub_r = b * 256 + c128 * 128
                for j in range(ndir):
                    dd = dirs[j]
                    rev = dd["rev"]
                    for w in range(8):
                        kb.dma("sync", dt_sb[:, w * TOKB:(w + 1) * TOKB],
                               dd["dt"][w * 512 + sub_r:w * 512 + sub_r + P, :],
                               waits=[SD, VD, GD], inc=LD)
                        kb.dma("sync", u_sb[:, w * TOKB:(w + 1) * TOKB],
                               dd["u"][w * 512 + sub_r:w * 512 + sub_r + P, :],
                               waits=[SD, VD], inc=LD)
                    dtu_ld = LD.v
                    segs = range(NSEG - 1, -1, -1) if rev else range(NSEG)
                    bc_ld = {}
                    b_gd = {}
                    scan_vd = {}
                    ym_vd = {}
                    carry_vd = {}
                    prev_scan = {}
                    prev_ym = {}
                    prev_gd = {}

                    def load_bc(s, sg):
                        slot = s % 2
                        for half in range(2):
                            w = sg * 2 + half
                            row = w * 64 + b * 32
                            hs = slice(half * TOKB, (half + 1) * TOKB)
                            if s - 2 in b_gd:
                                wts = [(GD, b_gd[s - 2]), (VD, ym_vd[s - 2])]
                            elif s - 2 + NS in prev_gd:
                                wts = [(GD, prev_gd[s - 2 + NS]),
                                       (VD, prev_ym[s - 2 + NS])]
                            else:
                                wts = [(GD, GD.v), (VD, VD.v)]
                            kb.dma("sync", Bbc_sb[:, slot, hs],
                                   dd["bc"][row + s:row + s + 1, :]
                                   .broadcast_to((P, TOKB)),
                                   waits=wts, inc=LD)
                            kb.dma("sync", Cbc_sb[:, slot, hs],
                                   dd["bc"][row + 16 + s:row + 16 + s + 1, :]
                                   .broadcast_to((P, TOKB)),
                                   waits=wts, inc=LD)
                        bc_ld[s] = LD.v

                    for si, sg in enumerate(segs):
                        gs = slice(sg * SEG, (sg + 1) * SEG)
                        kb.step("vector", lambda e, gs=gs: e.tensor_mul(
                            dtu_sb[:, :], dt_sb[:, gs], u_sb[:, gs]),
                            waits=[(LD, dtu_ld), VD, GD], inc=VD)
                        dtu_vd = VD.v
                        prev_scan = dict(scan_vd)
                        prev_ym = dict(ym_vd)
                        prev_gd = dict(b_gd)
                        bc_ld.clear(); b_gd.clear(); scan_vd.clear(); ym_vd.clear()
                        load_bc(0, sg)
                        for s in range(NS):
                            slot = s % 2
                            if s + 1 < NS:
                                load_bc(s + 1, sg)
                            aw = [(LD, dtu_ld), (VD, an_vd)]
                            if s - 2 in scan_vd:
                                aw.append((VD, scan_vd[s - 2]))
                            elif s - 2 + NS in prev_scan:
                                aw.append((VD, prev_scan[s - 2 + NS]))
                            kb.step("scalar",
                                    lambda e, j=j, c128=c128, s=s, gs=gs, slot=slot:
                                    e.activation(
                                        out=a_sb[:, slot, :], in_=dt_sb[:, gs],
                                        func=AF.Exp,
                                        scale=An_sb[:, j * 32 + c128 * NS + s:
                                                    j * 32 + c128 * NS + s + 1]),
                                    waits=aw, inc=SD)
                            a_sd = SD.v
                            bw = [(LD, bc_ld[s]), (VD, dtu_vd)]
                            if s - 2 in scan_vd:
                                bw.append((VD, scan_vd[s - 2]))
                            elif s - 2 + NS in prev_scan:
                                bw.append((VD, prev_scan[s - 2 + NS]))
                            kb.step("gpsimd", lambda e, slot=slot: e.tensor_mul(
                                b_sb[:, slot, :], dtu_sb[:, :], Bbc_sb[:, slot, :]),
                                waits=bw, inc=GD)
                            b_gd[s] = GD.v

                            def scan(e, si=si, s=s, rev=rev, slot=slot):
                                init = 0.0 if si == 0 else carry[:, s:s + 1]
                                if rev:
                                    return e.tensor_tensor_scan(
                                        out=h_sb[:, ::-1],
                                        data0=a_sb[:, slot, ::-1],
                                        data1=b_sb[:, slot, ::-1],
                                        initial=init, op0=ALU.mult, op1=ALU.add)
                                return e.tensor_tensor_scan(
                                    out=h_sb[:, :], data0=a_sb[:, slot, :],
                                    data1=b_sb[:, slot, :],
                                    initial=init, op0=ALU.mult, op1=ALU.add)
                            sw = [(SD, a_sd), (GD, b_gd[s])]
                            if si > 0 and s in carry_vd:
                                sw.append((VD, carry_vd[s]))
                            kb.step("vector", scan, waits=sw, inc=VD)
                            scan_vd[s] = VD.v
                            if si < NSEG - 1:
                                cc = slice(0, 1) if rev else slice(SEG - 1, SEG)
                                kb.step("vector", lambda e, s=s, cc=cc:
                                        e.tensor_copy(out=carry[:, s:s + 1],
                                                      in_=h_sb[:, cc]),
                                        waits=[VD], inc=VD)
                                carry_vd[s] = VD.v
                            kb.step("vector", lambda e, slot=slot: e.tensor_mul(
                                ym_sb[:, :], h_sb[:, :], Cbc_sb[:, slot, :]),
                                waits=[VD], inc=VD)
                            if s == 0:
                                kb.step("vector", lambda e, gs=gs: e.tensor_copy(
                                    out=y_sb[:, gs], in_=ym_sb[:, :]),
                                    waits=[VD], inc=VD)
                            else:
                                kb.step("vector", lambda e, gs=gs: e.tensor_add(
                                    y_sb[:, gs], y_sb[:, gs], ym_sb[:, :]),
                                    waits=[VD], inc=VD)
                            ym_vd[s] = VD.v
                        kb.step("vector", lambda e, j=j, c128=c128, gs=gs:
                                e.tensor_scalar(
                                    out=du_sb[:, :], in0=u_sb[:, gs],
                                    scalar1=Dc_sb[:, j * 2 + c128:j * 2 + c128 + 1],
                                    scalar2=None, op0=ALU.mult),
                                waits=[VD], inc=VD)
                        kb.step("vector", lambda e, gs=gs: e.tensor_add(
                            y_sb[:, gs], y_sb[:, gs], du_sb[:, :]),
                            waits=[VD], inc=VD)
                        if ndir > 1:
                            if j == 0:
                                kb.step("vector", lambda e, gs=gs: e.tensor_copy(
                                    out=ya_sb[:, gs], in_=y_sb[:, gs]),
                                    waits=[VD], inc=VD)
                            else:
                                kb.step("vector", lambda e, gs=gs: e.tensor_add(
                                    ya_sb[:, gs], ya_sb[:, gs], y_sb[:, gs]),
                                    waits=[VD], inc=VD)
                ysrc = ya_sb if ndir > 1 else y_sb
                for sg in range(NSEG):
                    gs = slice(sg * SEG, (sg + 1) * SEG)
                    for half in range(2):
                        w = sg * 2 + half
                        hs = slice(half * TOKB, (half + 1) * TOKB)
                        kb.dma("sync", zs_sb[:, hs],
                               zs_src[w * 512 + sub_r:w * 512 + sub_r + P, :],
                               waits=[VD, ST], inc=LD)
                    kb.step("vector", lambda e, gs=gs: e.tensor_mul(
                        ybf_sb[:, :], ysrc[:, gs], zs_sb[:, :]),
                        waits=[VD, LD, ST], inc=VD)
                    for half in range(2):
                        w = sg * 2 + half
                        hs = slice(half * TOKB, (half + 1) * TOKB)
                        kb.dma("sync",
                               y_dst[w * 512 + sub_r:w * 512 + sub_r + P, :],
                               ybf_sb[:, hs], waits=[VD], inc=ST)

    def sec_back(layer):
        if layer == 0:
            y_src, wout_src = y0_o, wout0
        else:
            y_src, wout_src = y1_o, wout1

        wbuf = kb.sbuf("bwbuf", [P, 16, D], BF16)
        id_sb = kb.sbuf("bid", [P, P], BF16)
        x_t = kb.sbuf("bx_t", [P, D], F32)
        res_t = kb.sbuf("bres", [P, D], F32)
        scr = kb.sbuf("bscr", [P, D], F32)
        cols = kb.sbuf("bcols", [P, 8], F32)
        x1_sb = kb.sbuf("bx1", [P, 16, D], F32)

        ps_o = [kb.psum(f"bpso{i}", [P, 512], F32) for i in range(2)]
        ps_tr = [kb.psum(f"bpstr{i}", [P, 1024], BF16) for i in range(2)]
        ps_g = kb.psum("bpsg", [P, 512], F32)
        ps_h = kb.psum("bpsh", [P, 512], F32)
        ps_y = [kb.psum(f"bpsy{i}", [P, 512], F32) for i in range(2)]

        for a in range(16):
            kb.dma("gpsimd", wbuf[:, a, :], wout_src[a * P:(a + 1) * P, :],
                   waits=[MM], inc=LD)
        kb.dma("sync", id_sb[:, :], ident[:, :], inc=LD)
        static_ld = LD.v

        with kb.scope():
            y_blk = kb.sbuf("by_blk", [P, 2, 16, P], BF16)
            obank = [ps_o[0], ps_o[1], ps_y[0], ps_y[1]]
            oguard = [0, 0]
            yguard = [0, 0]
            for b in range(2):
                for m in range(8):
                    ti = b * 8 + m
                    mi = (b * 8 + m) % 2
                    for a in range(16):
                        r0 = (a // 2) * 512 + b * 256 + (a % 2) * 128
                        kb.dma("sync", y_blk[:, mi, a, :],
                               y_src[r0:r0 + P, m * P:(m + 1) * P],
                               waits=[(LD, static_ld), (MM, yguard[mi])], inc=LD)
                    yld = LD.v
                    if layer == 0:
                        kb.dma("sync", x_t[:, :],
                               xq[b, 3 + m * P:3 + (m + 1) * P, :],
                               waits=[VD], inc=LD)
                    else:
                        kb.dma("sync", x_t[:, :],
                               x2loc[b * TOKB + m * P:b * TOKB + (m + 1) * P, :],
                               waits=[VD], inc=LD)

                    def mmop(e, mi=mi, yld=yld):
                        last = None
                        for a in range(16):
                            for nn in range(2):
                                last = e.matmul(obank[2 * mi + nn][:, :],
                                                y_blk[:, mi, a, :],
                                                wbuf[:, a, nn * 512:(nn + 1) * 512],
                                                start=(a == 0), stop=(a == 15))
                        return last
                    kb.step("tensor", mmop,
                            waits=[LD, (EV, oguard[mi])], inc=MM)
                    yguard[mi] = MM.v
                    rscale = 2.0 if layer == 0 else 1.0
                    for nn in range(2):
                        kb.step("vector", lambda e, nn=nn, mi=mi, rscale=rscale:
                                e.scalar_tensor_tensor(
                                    out=res_t[:, nn * 512:(nn + 1) * 512],
                                    in0=x_t[:, nn * 512:(nn + 1) * 512],
                                    scalar=rscale, in1=obank[2 * mi + nn][:, :],
                                    op0=ALU.mult, op1=ALU.add),
                                waits=[MM, LD], inc=EV)
                    oguard[mi] = EV.v
                    emit_ln_tile(kb, res_t, D, scr, cols, VD, SD,
                                 out_f32=x1_sb[:, ti, :], waits=[EV])

        if upto == "back0a":
            for b in range(2):
                for m in range(8):
                    ti = b * 8 + m
                    row = b * TOKB + m * P
                    kb.dma("sync", dbg_out[row:row + P, :], x1_sb[:, ti, :],
                           waits=[VD, SD], inc=ST)
            return
        S = dict(wbuf=wbuf, id_sb=id_sb, x_t=x_t, res_t=res_t,
                 scr=scr, cols=cols, x1_sb=x1_sb, ps_tr=ps_tr, ps_g=ps_g,
                 ps_h=ps_h, ps_y=ps_y)
        if layer == 0:
            sec_ffn0(kb, S)
        else:
            sec_ffn1(kb, S)

    def sec_ffn0(kb2, S):
        NE, HID = 8, 1536
        wbuf, id_sb = S["wbuf"], S["id_sb"]
        x_t, res_t, scr, cols = S["x_t"], S["res_t"], S["scr"], S["cols"]
        x1_sb = S["x1_sb"]
        x1fm = kb.sbuf("c_x1fm", [P, 8, 512], BF16)
        x1bf = kb.sbuf("c_x1bf", [P, D], BF16)
        ps_tr, ps_g, ps_h, ps_y = S["ps_tr"], S["ps_g"], S["ps_h"], S["ps_y"]
        w13 = kb.sbuf("c_w13", [P, 2, 8, HID], BF16)
        hbank = [ps_h, ps_g]
        hguard = [0, 0]
        wr_sb = kb.sbuf("c_wr", [P, 8, NE], BF16)
        glog_cm = kb.sbuf("c_glcm", [NE, P], BF16)
        glog = kb.sbuf("c_gl", [P, NE], F32)
        gcol = kb.sbuf("c_gc", [P, 8], F32)
        gtmp = kb.sbuf("c_gt", [P, NE], F32)
        comb = kb.sbuf("c_comb", [P, 16, NE], F32)
        hsil = kb.sbuf("c_hsil", [P, 12, 512], BF16)
        gbuf = kb.sbuf("c_gbuf", [P, 12, 512], BF16)
        yff = kb.sbuf("c_yff", [P, 4, D], F32)

        for k in range(8):
            kb.dma("gpsimd", wr_sb[:, k, :], wr0[k * P:(k + 1) * P, :],
                   waits=[MM], inc=LD)

        guards = {"h": 0, "g": 0, "y": 0}
        for c in range(4):
            b, c2 = c // 2, c % 2
            for tt in range(4):
                ti = b * 8 + c2 * 4 + tt
                kb.step("vector", lambda e, ti=ti: e.tensor_copy(
                    out=x1bf[:, :], in_=x1_sb[:, ti, :]),
                    waits=[VD, SD, TR], inc=VD)
                for f in range(8):
                    slot = f % 2

                    def tr(e, f=f, slot=slot):
                        return e.transpose(ps_tr[slot][:, 0:P],
                                           x1bf[:, f * P:(f + 1) * P], id_sb[:, :])
                    kb.step("tensor", tr, waits=[VD, EV], inc=TR)

                    def trev(e, f=f, tt=tt, slot=slot):
                        return e.tensor_copy(out=x1fm[:, f, tt * P:(tt + 1) * P],
                                             in_=ps_tr[slot][:, 0:P])
                    kb.step("vector", trev, waits=[TR], inc=EV)

            def gmm(e):
                last = None
                for k in range(8):
                    last = e.matmul(ps_g[0:NE, :], wr_sb[:, k, :], x1fm[:, k, :],
                                    start=(k == 0), stop=(k == 7))
                return last
            kb.step("tensor", gmm, waits=[EV, VD], inc=MM)
            for tt in range(4):
                ti = b * 8 + c2 * 4 + tt
                slot = tt % 2
                kb.step("vector", lambda e, tt=tt: e.tensor_copy(
                    out=glog_cm[:, :], in_=ps_g[0:NE, tt * P:(tt + 1) * P]),
                    waits=[MM, TR], inc=EV)
                if tt == 3:
                    hguard[1] = EV.v

                def gtr(e, slot=slot):
                    return e.transpose(ps_tr[slot][:, 0:NE],
                                       glog_cm[0:NE, 0:P],
                                       id_sb[0:NE, 0:NE])
                kb.step("tensor", gtr, waits=[EV, VD], inc=TR)
                kb.step("vector", lambda e, slot=slot: e.tensor_copy(
                    out=glog[:, 0:NE], in_=ps_tr[slot][:, 0:NE]),
                    waits=[TR], inc=EV)
                emit_gate_top2(kb, glog, gcol, gtmp, comb[:, ti, 0:NE], VD, SD,
                               waits=[EV])
            for e_i in range(NE):
                for k in range(8):
                    kb.dma("gpsimd", w13[:, 0, k, :], w10[e_i, k * P:(k + 1) * P, :],
                           waits=[(MM, guards["h"])], inc=LD)
                w1_ld = LD.v
                for k in range(8):
                    kb.dma("gpsimd", w13[:, 1, k, :], w30[e_i, k * P:(k + 1) * P, :],
                           waits=[(MM, guards["g"])], inc=LD)
                w3_ld = LD.v
                for k in range(12):
                    kb.dma("gpsimd", wbuf[:, k, :], w20[e_i, k * P:(k + 1) * P, :],
                           waits=[(MM, guards["y"])], inc=LD)
                w2_ld = LD.v
                for hm in range(12):
                    hb_ = hm % 2

                    def hmm(e, hm=hm, hb_=hb_):
                        last = None
                        for k in range(8):
                            last = e.matmul(hbank[hb_][0:P, :],
                                            w13[:, 0, k, hm * P:(hm + 1) * P],
                                            x1fm[:, k, :],
                                            start=(k == 0), stop=(k == 7))
                        return last
                    kb.step("tensor", hmm,
                            waits=[(LD, w1_ld), (EV, hguard[hb_])], inc=MM)
                    mmv = MM.v
                    kb.step("scalar", lambda e, hm=hm, hb_=hb_: e.activation(
                        out=hsil[:, hm, :], in_=hbank[hb_][0:P, :], func=AF.Silu),
                        waits=[(MM, mmv)], inc=EV)
                    hguard[hb_] = EV.v
                guards["h"] = MM.v
                for hm in range(12):
                    hb_ = hm % 2

                    def gmm2(e, hm=hm, hb_=hb_):
                        last = None
                        for k in range(8):
                            last = e.matmul(hbank[hb_][0:P, :],
                                            w13[:, 1, k, hm * P:(hm + 1) * P],
                                            x1fm[:, k, :],
                                            start=(k == 0), stop=(k == 7))
                        return last
                    kb.step("tensor", gmm2,
                            waits=[(LD, w3_ld), (EV, hguard[hb_])], inc=MM)
                    mmv = MM.v
                    kb.step("scalar", lambda e, hm=hm, hb_=hb_: e.activation(
                        out=gbuf[:, hm, :], in_=hbank[hb_][0:P, :], func=AF.Copy),
                        waits=[(MM, mmv)], inc=EV)
                    hguard[hb_] = EV.v
                    kb.step("vector", lambda e, hm=hm: e.tensor_mul(
                        gbuf[:, hm, :], hsil[:, hm, :], gbuf[:, hm, :]),
                        waits=[EV], inc=VD)
                guards["g"] = MM.v
                for tt in range(4):
                    ti = b * 8 + c2 * 4 + tt

                    def ymm(e, tt=tt):
                        last = None
                        for hk in range(12):
                            for nn in range(2):
                                last = e.matmul(
                                    ps_y[nn][:, :],
                                    gbuf[:, hk, tt * P:(tt + 1) * P],
                                    wbuf[:, hk, nn * 512:(nn + 1) * 512],
                                    start=(hk == 0), stop=(hk == 11))
                        return last
                    kb.step("tensor", ymm, waits=[VD, (LD, w2_ld), EV], inc=MM)
                    guards["y"] = MM.v
                    for nn in range(2):
                        sl = slice(nn * 512, (nn + 1) * 512)
                        if e_i == 0:
                            kb.step("vector",
                                    lambda e, tt=tt, ti=ti, nn=nn, sl=sl, e_i=e_i:
                                    e.tensor_scalar(
                                        out=yff[:, tt, sl], in0=ps_y[nn][:, :],
                                        scalar1=comb[:, ti, e_i:e_i + 1],
                                        scalar2=None, op0=ALU.mult),
                                    waits=[MM], inc=EV)
                        else:
                            kb.step("vector",
                                    lambda e, tt=tt, ti=ti, nn=nn, sl=sl, e_i=e_i:
                                    e.scalar_tensor_tensor(
                                        out=yff[:, tt, sl], in0=ps_y[nn][:, :],
                                        scalar=comb[:, ti, e_i:e_i + 1],
                                        in1=yff[:, tt, sl],
                                        op0=ALU.mult, op1=ALU.add),
                                    waits=[MM], inc=EV)
            for tt in range(4):
                ti = b * 8 + c2 * 4 + tt
                kb.step("vector", lambda e, tt=tt, ti=ti: e.tensor_add(
                    res_t[:, :], x1_sb[:, ti, :], yff[:, tt, :]),
                    waits=[EV, VD, SD], inc=VD)
                emit_ln_tile(kb, res_t, D, scr, cols, VD, SD, out_f32=x_t,
                             waits=[VD, ST])
                row = b * TOKB + (c2 * 4 + tt) * P
                kb.dma("sync", x2loc[row:row + P, :], x_t[:, :],
                       waits=[VD, SD], inc=ST)
        for b in range(2):
            kb.dma("sync", hal_i[b * 6:b * 6 + 3, :],
                   x2loc[b * TOKB:b * TOKB + 3, :], waits=[ST], inc=ST)
            kb.dma("sync", hal_i[b * 6 + 3:b * 6 + 6, :],
                   x2loc[(b + 1) * TOKB - 3:(b + 1) * TOKB, :],
                   waits=[ST], inc=ST)
        if upto == "x2":
            for row in range(0, 2 * TOKB, P):
                kb.dma("sync", dbg_out[row:row + P, :], x2loc[row:row + P, :],
                       waits=[ST], inc=ST)

    def sec_ffn1(kb2, S):
        NE, INTER, NSH = 16, 512, 1024
        wbuf, id_sb = S["wbuf"], S["id_sb"]
        x_t, res_t, scr, cols = S["x_t"], S["res_t"], S["scr"], S["cols"]
        x3_sb = S["x1_sb"]
        x3fm = kb.sbuf("f_x3fm", [P, 8, 512], BF16)
        x3bf = kb.sbuf("f_x3bf", [P, D], BF16)
        ps_tr, ps_g, ps_h, ps_y = S["ps_tr"], S["ps_g"], S["ps_h"], S["ps_y"]
        w13 = kb.sbuf("f_w13", [P, 2, 8, NSH], BF16)
        hbank = [ps_h, ps_g]
        hguard = [0, 0]
        gw_sb = kb.sbuf("f_gw", [P, 8, NE], BF16)
        gbb_sb = kb.sbuf("f_gbb", [P, NE], F32)
        glog_cm = kb.sbuf("f_glcm", [NE, P], BF16)
        glog = kb.sbuf("f_gl", [P, NE], F32)
        gexp = kb.sbuf("f_ge", [P, NE], F32)
        gorig = kb.sbuf("f_go", [P, NE], F32)
        gtmp = kb.sbuf("f_gt", [P, NE], F32)
        gp4 = kb.sbuf("f_gp4", [P, 12], F32)
        gcol = kb.sbuf("f_gc", [P, 8], F32)
        comb = kb.sbuf("f_comb", [P, 16, NE], F32)
        hsil = kb.sbuf("f_hsil", [P, 8, 512], BF16)
        gbuf = kb.sbuf("f_gbuf", [P, 8, 512], BF16)
        yff = kb.sbuf("f_yff", [P, 4, D], F32)
        be2_sb = kb.sbuf("f_be2", [P, D], F32)
        bsml = kb.sbuf("f_bsml", [P, 2 * NE * 4 + 16 + D], F32)

        for k in range(8):
            kb.dma("gpsimd", gw_sb[:, k, :], gw1[k * P:(k + 1) * P, :],
                   waits=[MM], inc=LD)
        kb.dma("sync", gbb_sb[:, :], gb1[:, :], inc=LD)
        for e_i in range(NE):
            kb.dma("sync", bsml[:, e_i * 4:(e_i + 1) * 4], be1[e_i, :, :], inc=LD)
            kb.dma("sync", bsml[:, 64 + e_i * 4:64 + (e_i + 1) * 4],
                   be3[e_i, :, :], inc=LD)
        kb.dma("sync", bsml[:, 128:136], bs1[:, :], inc=LD)
        kb.dma("sync", bsml[:, 136:144], bs3[:, :], inc=LD)
        kb.dma("sync", bsml[:, 144:144 + D], bs2[:, :], inc=LD)

        guards = {"h": [0, 0], "g": [0, 0], "y": [0, 0]}
        for c in range(4):
            b, c2 = c // 2, c % 2
            for tt in range(4):
                ti = b * 8 + c2 * 4 + tt
                kb.step("vector", lambda e, ti=ti: e.tensor_copy(
                    out=x3bf[:, :], in_=x3_sb[:, ti, :]),
                    waits=[VD, SD, TR], inc=VD)
                for f in range(8):
                    slot = f % 2

                    def tr(e, f=f, slot=slot):
                        return e.transpose(ps_tr[slot][:, 0:P],
                                           x3bf[:, f * P:(f + 1) * P], id_sb[:, :])
                    kb.step("tensor", tr, waits=[VD, EV], inc=TR)

                    def trev(e, f=f, tt=tt, slot=slot):
                        return e.tensor_copy(out=x3fm[:, f, tt * P:(tt + 1) * P],
                                             in_=ps_tr[slot][:, 0:P])
                    kb.step("vector", trev, waits=[TR], inc=EV)

            def gmm(e):
                last = None
                for k in range(8):
                    last = e.matmul(ps_g[0:NE, :], gw_sb[:, k, :], x3fm[:, k, :],
                                    start=(k == 0), stop=(k == 7))
                return last
            kb.step("tensor", gmm, waits=[EV, VD], inc=MM)
            for tt in range(4):
                ti = b * 8 + c2 * 4 + tt
                slot = tt % 2
                kb.step("vector", lambda e, tt=tt: e.tensor_copy(
                    out=glog_cm[:, :], in_=ps_g[0:NE, tt * P:(tt + 1) * P]),
                    waits=[MM, TR], inc=EV)
                if tt == 3:
                    hguard[1] = EV.v

                def gtr(e, slot=slot):
                    return e.transpose(ps_tr[slot][:, 0:NE],
                                       glog_cm[0:NE, 0:P],
                                       id_sb[0:NE, 0:NE])
                kb.step("tensor", gtr, waits=[EV, VD], inc=TR)
                kb.step("vector", lambda e, slot=slot: e.tensor_copy(
                    out=glog[:, 0:NE], in_=ps_tr[slot][:, 0:NE]),
                    waits=[TR], inc=EV)
                emit_gate_l1(kb, glog, gexp, gorig, gtmp, gp4, gcol, gbb_sb,
                             comb[:, ti, 0:NE], VD, SD, waits=[EV])
            for e_i in range(NE):
                es = e_i % 2
                for k in range(8):
                    kb.dma("gpsimd", w13[:, 0, k, es * INTER:(es + 1) * INTER],
                           we1[e_i, k * P:(k + 1) * P, :],
                           waits=[(MM, guards["h"][es])], inc=LD)
                w1_ld = LD.v
                for k in range(8):
                    kb.dma("gpsimd", w13[:, 1, k, es * INTER:(es + 1) * INTER],
                           we3[e_i, k * P:(k + 1) * P, :],
                           waits=[(MM, guards["g"][es])], inc=LD)
                w3_ld = LD.v
                for k in range(4):
                    kb.dma("gpsimd", wbuf[:, es * 4 + k, :],
                           we2[e_i, k * P:(k + 1) * P, :],
                           waits=[(MM, guards["y"][es])], inc=LD)
                w2_ld = LD.v
                kb.dma("sync", be2_sb[:, :], be2[e_i, :, :], waits=[VD], inc=LD)
                be2_ld = LD.v
                for hm in range(4):
                    hb_ = hm % 2

                    def hmm(e, hm=hm, hb_=hb_, es=es):
                        last = None
                        for k in range(8):
                            last = e.matmul(hbank[hb_][0:P, :],
                                            w13[:, 0, k, es * INTER + hm * P:
                                                es * INTER + (hm + 1) * P],
                                            x3fm[:, k, :],
                                            start=(k == 0), stop=(k == 7))
                        return last
                    kb.step("tensor", hmm,
                            waits=[(LD, w1_ld), (EV, hguard[hb_])], inc=MM)
                    mmv = MM.v
                    kb.step("scalar", lambda e, hm=hm, hb_=hb_, e_i=e_i:
                            e.activation(
                                out=hsil[:, hm, :], in_=hbank[hb_][0:P, :],
                                func=AF.Silu,
                                bias=bsml[:, e_i * 4 + hm:e_i * 4 + hm + 1]),
                            waits=[(MM, mmv)], inc=EV)
                    hguard[hb_] = EV.v
                guards["h"][es] = MM.v
                for hm in range(4):
                    hb_ = hm % 2

                    def gmm2(e, hm=hm, hb_=hb_, es=es):
                        last = None
                        for k in range(8):
                            last = e.matmul(hbank[hb_][0:P, :],
                                            w13[:, 1, k, es * INTER + hm * P:
                                                es * INTER + (hm + 1) * P],
                                            x3fm[:, k, :],
                                            start=(k == 0), stop=(k == 7))
                        return last
                    kb.step("tensor", gmm2,
                            waits=[(LD, w3_ld), (EV, hguard[hb_])], inc=MM)
                    mmv = MM.v
                    kb.step("vector", lambda e, hm=hm, hb_=hb_, e_i=e_i:
                            e.tensor_scalar(
                                out=gbuf[:, hm, :], in0=hbank[hb_][0:P, :],
                                scalar1=bsml[:, 64 + e_i * 4 + hm:
                                             64 + e_i * 4 + hm + 1],
                                scalar2=None, op0=ALU.add),
                            waits=[(MM, mmv), VD], inc=EV)
                    hguard[hb_] = EV.v
                    kb.step("vector", lambda e, hm=hm: e.tensor_mul(
                        gbuf[:, hm, :], hsil[:, hm, :], gbuf[:, hm, :]),
                        waits=[EV], inc=VD)
                guards["g"][es] = MM.v
                for tt in range(4):
                    ti = b * 8 + c2 * 4 + tt

                    def ymm(e, tt=tt, es=es):
                        last = None
                        for hk in range(4):
                            for nn in range(2):
                                last = e.matmul(
                                    ps_y[nn][:, :],
                                    gbuf[:, hk, tt * P:(tt + 1) * P],
                                    wbuf[:, es * 4 + hk, nn * 512:(nn + 1) * 512],
                                    start=(hk == 0), stop=(hk == 3))
                        return last
                    kb.step("tensor", ymm, waits=[VD, (LD, w2_ld), EV], inc=MM)
                    guards["y"][es] = MM.v
                    for nn in range(2):
                        sl = slice(nn * 512, (nn + 1) * 512)
                        if e_i == 0:
                            kb.step("vector",
                                    lambda e, tt=tt, ti=ti, nn=nn, sl=sl, e_i=e_i:
                                    e.tensor_scalar(
                                        out=yff[:, tt, sl], in0=ps_y[nn][:, :],
                                        scalar1=comb[:, ti, e_i:e_i + 1],
                                        scalar2=None, op0=ALU.mult),
                                    waits=[MM], inc=EV)
                        else:
                            kb.step("vector",
                                    lambda e, tt=tt, ti=ti, nn=nn, sl=sl, e_i=e_i:
                                    e.scalar_tensor_tensor(
                                        out=yff[:, tt, sl], in0=ps_y[nn][:, :],
                                        scalar=comb[:, ti, e_i:e_i + 1],
                                        in1=yff[:, tt, sl],
                                        op0=ALU.mult, op1=ALU.add),
                                    waits=[MM], inc=EV)
                        kb.step("vector",
                                lambda e, tt=tt, ti=ti, sl=sl, e_i=e_i:
                                e.scalar_tensor_tensor(
                                    out=yff[:, tt, sl], in0=be2_sb[:, sl],
                                    scalar=comb[:, ti, e_i:e_i + 1],
                                    in1=yff[:, tt, sl],
                                    op0=ALU.mult, op1=ALU.add),
                                waits=[EV, LD], inc=VD)
            for k in range(8):
                kb.dma("gpsimd", w13[:, 0, k, :], ws1[k * P:(k + 1) * P, :],
                       waits=[MM], inc=LD)
                kb.dma("gpsimd", w13[:, 1, k, :], ws3[k * P:(k + 1) * P, :],
                       waits=[MM], inc=LD)
            for hm in range(8):
                hb_ = hm % 2

                def hmms(e, hm=hm, hb_=hb_):
                    last = None
                    for k in range(8):
                        last = e.matmul(hbank[hb_][0:P, :],
                                        w13[:, 0, k, hm * P:(hm + 1) * P],
                                        x3fm[:, k, :], start=(k == 0),
                                        stop=(k == 7))
                    return last
                kb.step("tensor", hmms, waits=[LD, (EV, hguard[hb_])], inc=MM)
                mmv = MM.v
                kb.step("scalar", lambda e, hm=hm, hb_=hb_: e.activation(
                    out=hsil[:, hm, :], in_=hbank[hb_][0:P, :], func=AF.Silu,
                    bias=bsml[:, 128 + hm:128 + hm + 1]),
                    waits=[(MM, mmv)], inc=EV)
                hguard[hb_] = EV.v
                guards["h"][0] = guards["h"][1] = MM.v
            for hm in range(8):
                hb_ = hm % 2

                def gmms(e, hm=hm, hb_=hb_):
                    last = None
                    for k in range(8):
                        last = e.matmul(hbank[hb_][0:P, :],
                                        w13[:, 1, k, hm * P:(hm + 1) * P],
                                        x3fm[:, k, :], start=(k == 0),
                                        stop=(k == 7))
                    return last
                kb.step("tensor", gmms, waits=[LD, (EV, hguard[hb_])], inc=MM)
                mmv = MM.v
                kb.step("vector", lambda e, hm=hm, hb_=hb_: e.tensor_scalar(
                    out=gbuf[:, hm, :], in0=hbank[hb_][0:P, :],
                    scalar1=bsml[:, 136 + hm:136 + hm + 1], scalar2=None,
                    op0=ALU.add),
                    waits=[(MM, mmv), VD], inc=EV)
                kb.step("vector", lambda e, hm=hm: e.tensor_mul(
                    gbuf[:, hm, :], hsil[:, hm, :], gbuf[:, hm, :]),
                    waits=[EV], inc=VD)
                guards["g"][0] = guards["g"][1] = MM.v
            for k in range(8):
                kb.dma("gpsimd", wbuf[:, k, :], ws2[k * P:(k + 1) * P, :],
                       waits=[MM], inc=LD)
            for tt in range(4):
                def ymms(e, tt=tt):
                    last = None
                    for hk in range(8):
                        for nn in range(2):
                            last = e.matmul(ps_y[nn][:, :],
                                            gbuf[:, hk, tt * P:(tt + 1) * P],
                                            wbuf[:, hk, nn * 512:(nn + 1) * 512],
                                            start=(hk == 0), stop=(hk == 7))
                    return last
                kb.step("tensor", ymms, waits=[VD, EV, LD], inc=MM)
                guards["y"][0] = guards["y"][1] = MM.v
                for nn in range(2):
                    sl = slice(nn * 512, (nn + 1) * 512)
                    kb.step("vector", lambda e, tt=tt, sl=sl, nn=nn:
                            e.scalar_tensor_tensor(
                                out=yff[:, tt, sl], in0=ps_y[nn][:, :],
                                scalar=1.0, in1=yff[:, tt, sl],
                                op0=ALU.mult, op1=ALU.add),
                            waits=[MM, VD], inc=EV)
                kb.step("vector", lambda e, tt=tt: e.tensor_add(
                    yff[:, tt, :], yff[:, tt, :], bsml[:, 144:144 + D]),
                    waits=[EV], inc=VD)
            for tt in range(4):
                ti = b * 8 + c2 * 4 + tt
                kb.step("vector", lambda e, tt=tt, ti=ti: e.tensor_add(
                    res_t[:, :], x3_sb[:, ti, :], yff[:, tt, :]),
                    waits=[EV, VD, SD], inc=VD)
                emit_ln_tile(kb, res_t, D, scr, cols, VD, SD, out_f32=x_t,
                             waits=[VD, ST])
                row = b * TOKB + (c2 * 4 + tt) * P
                kb.dma("sync", x4_out[row:row + P, :], x_t[:, :],
                       waits=[VD, SD], inc=ST)

    def sec_x2h():
        halg_sb = kb.sbuf("h_halg", [96, D], BF16)
        hsel_sb = kb.sbuf("h_hsel", [96, 12], BF16)
        hout_sb = kb.sbuf("h_hout", [12, D], F32)
        ps_hal = kb.psum("h_ps", [P, 1024], F32)
        for b in range(2):
            kb.dma("gpsimd", x2h[b, 3:3 + TOKB, :],
                   x2loc[b * TOKB:(b + 1) * TOKB, :], waits=[ST], inc=ST)
        kb.dma("gpsimd", halg_sb[:, :], hal_g.ap(), waits=[(CC, CC.v)], inc=LD)
        kb.dma("gpsimd", hsel_sb[:, :], hsel[:, :], inc=LD)

        def selmm(e):
            last = None
            for nn in range(2):
                last = e.matmul(ps_hal[0:12, nn * 512:(nn + 1) * 512],
                                hsel_sb[:, :],
                                halg_sb[:, nn * 512:(nn + 1) * 512],
                                start=True, stop=True)
            return last
        kb.step("tensor", selmm, waits=[LD], inc=MM)
        kb.step("vector", lambda e: e.tensor_copy(out=hout_sb[:, :],
                                                  in_=ps_hal[0:12, 0:D]),
                waits=[MM], inc=VD)
        for b in range(2):
            kb.dma("sync", x2h[b, 0:3, :], hout_sb[b * 6:b * 6 + 3, :],
                   waits=[VD], inc=ST)
            kb.dma("sync", x2h[b, 3 + TOKB:6 + TOKB, :],
                   hout_sb[b * 6 + 3:b * 6 + 6, :], waits=[VD], inc=ST)

    STAGES = ["front0", "coll0", "scan0", "back0a", "x2", "x4"]
    lvl = STAGES.index(upto)
    with kb.scope():
        sec_front(0)
    if lvl >= 1:
        coll("AllToAll", [u0_i, dt0_i, zs0_i], [u0_o, dt0_o, zs0_o], waits=[ST])
        coll("AllGather", [bc0_i], [bc0_g], waits=[ST])
    if lvl >= 2:
        with kb.scope():
            sec_scan_new(0)
        coll("AllToAll", [y0_i], [y0_o], waits=[ST])
    if lvl >= 3:
        with kb.scope():
            sec_back(0)
        if lvl >= 4:
            coll("AllGather", [hal_i], [hal_g], waits=[ST])
    kb._hal_cc = CC.v
    if lvl >= 5:
        with kb.scope():
            sec_x2h()
        with kb.scope():
            sec_front(1)
        coll("AllToAll", [uf_i, dtf_i, ub_i, dtb_i, zs1_i],
             [uf_o, dtf_o, ub_o, dtb_o, zs1_o], waits=[ST])
        coll("AllGather", [bcf_i, bcb_i], [bcf_g, bcb_g], waits=[ST])
        with kb.scope():
            sec_scan(1)
        coll("AllToAll", [y1_i], [y1_o], waits=[ST])
        with kb.scope():
            sec_back(1)
    kb.barrier_all()
    kb.barrier_wait("sync", ST)
    with aot_env.allow_ici_all_to_all():
        return kb.build()


def prep_fused(inp, consts):
    x = np.asarray(inp["L0_attn_x"])

    def vec16(v):
        return np.ascontiguousarray(np.asarray(v).reshape(16, P).T)

    def cwr(w):
        return np.ascontiguousarray(np.asarray(w).reshape(16, P, 4).transpose(1, 0, 2))

    shared = dict(
        ident=consts["ident_bf"],
        lnw_bc=np.tile(np.asarray(inp["L0_attn_ln_w"])[None, :], (P, 1)).astype(np.float32),
        lnb_bc=np.tile(np.asarray(inp["L0_attn_ln_b"])[None, :], (P, 1)).astype(np.float32),
        w_in0=np.asarray(inp["L0_attn_in_proj_w"]),
        convw0=cwr(inp["L0_attn_conv_w"]), convb0=vec16(inp["L0_attn_conv_b"]),
        xpw0=np.asarray(inp["L0_attn_x_proj_w"]),
        dtw0=np.asarray(inp["L0_attn_dt_proj_w"]),
        dtb0=vec16(inp["L0_attn_dt_proj_b"]),
        wout0=np.asarray(inp["L0_attn_out_proj_w"]),
        wr0=np.asarray(inp["L0_ffn_Wr"]),
        w10=np.asarray(inp["L0_ffn_W1"]), w20=np.asarray(inp["L0_ffn_W2"]),
        w30=np.asarray(inp["L0_ffn_W3"]),
        w_in1=np.ascontiguousarray(np.asarray(inp["L1_attn_in_proj_w"]).T),
        convwf=cwr(inp["L1_attn_conv_w"]), convbf=vec16(inp["L1_attn_conv_b"]),
        convwb=cwr(inp["L1_attn_conv_wb"]), convbb=vec16(inp["L1_attn_conv_bb"]),
        xpwf=np.ascontiguousarray(np.asarray(inp["L1_attn_x_proj_w"]).T),
        xpwb=np.ascontiguousarray(np.asarray(inp["L1_attn_x_proj_wb"]).T),
        dtwf=np.ascontiguousarray(np.asarray(inp["L1_attn_dt_proj_w"]).T),
        dtwb=np.ascontiguousarray(np.asarray(inp["L1_attn_dt_proj_wb"]).T),
        dtbf=vec16(inp["L1_attn_dt_proj_b"]), dtbb=vec16(inp["L1_attn_dt_proj_bb"]),
        wout1=np.ascontiguousarray(np.asarray(inp["L1_attn_out_proj_w"]).T),
        gw1=np.ascontiguousarray(np.asarray(inp["L1_ffn_gate_w"]).T),
        gb1=np.tile(np.asarray(inp["L1_ffn_gate_b"])[None, :], (P, 1)).astype(np.float32),
        we1=np.asarray(inp["L1_ffn_We1"]), we2=np.asarray(inp["L1_ffn_We2"]),
        we3=np.asarray(inp["L1_ffn_We3"]),
        be1=np.ascontiguousarray(
            np.asarray(inp["L1_ffn_be1"]).reshape(16, 4, P).transpose(0, 2, 1)),
        be3=np.ascontiguousarray(
            np.asarray(inp["L1_ffn_be3"]).reshape(16, 4, P).transpose(0, 2, 1)),
        be2=np.ascontiguousarray(
            np.broadcast_to(np.asarray(inp["L1_ffn_be2"])[:, None, :], (16, P, D))),
        ws1=np.asarray(inp["L1_ffn_ws1"]), ws2=np.asarray(inp["L1_ffn_ws2"]),
        ws3=np.asarray(inp["L1_ffn_ws3"]),
        bs1=np.ascontiguousarray(np.asarray(inp["L1_ffn_bs1"]).reshape(8, P).T),
        bs3=np.ascontiguousarray(np.asarray(inp["L1_ffn_bs3"]).reshape(8, P).T),
        bs2=np.tile(np.asarray(inp["L1_ffn_bs2"])[None, :], (P, 1)).astype(np.float32),
    )

    def acol(A_log, c0):
        return np.ascontiguousarray(
            np.asarray(A_log)[c0:c0 + 256].reshape(2, P, NS)
            .transpose(1, 0, 2).reshape(P, 2 * NS)).astype(np.float32)

    def dcol(Dv, c0):
        return np.ascontiguousarray(
            np.asarray(Dv)[c0:c0 + 256].reshape(2, P).T).astype(np.float32)

    in_maps = []
    for k in range(NCORES):
        t0 = k * TOKB
        c0 = k * 256
        xqk = np.zeros((2, TOKB + 3, D), np.float32)
        for b in range(2):
            lo = t0 - 3
            if lo < 0:
                xqk[b, 3:, :] = x[b, t0:t0 + TOKB]
            else:
                xqk[b, :, :] = x[b, lo:t0 + TOKB]
        mask3 = np.zeros((P, 3), np.float32) if k == 0 else np.ones((P, 3), np.float32)
        hsel = np.zeros((96, 12), np.float32)
        for b in range(2):
            if k > 0:
                for r in range(3):
                    hsel[(k - 1) * 12 + b * 6 + 3 + r, b * 6 + r] = 1.0
            if k < 7:
                for r in range(3):
                    hsel[(k + 1) * 12 + b * 6 + r, b * 6 + 3 + r] = 1.0
        m = dict(shared)
        m.update(
            xq=xqk, mask3=mask3, hsel=hsel,
            acol0=acol(inp["L0_attn_A_log"], c0), dcol0=dcol(inp["L0_attn_D_skip"], c0),
            acol1f=acol(inp["L1_attn_A_log"], c0), dcol1f=dcol(inp["L1_attn_D"], c0),
            acol1b=acol(inp["L1_attn_A_b_log"], c0), dcol1b=dcol(inp["L1_attn_D_b"], c0),
        )
        in_maps.append(m)
    return in_maps


def kernel_fused(inputs, upto="x4"):
    consts = make_consts()
    key = "FUSED_" + upto
    if key not in _BUILD_CACHE:
        _BUILD_CACHE[key] = build_fused(upto)
    nc = _BUILD_CACHE[key]
    res = run_bass_kernel_spmd(nc, prep_fused(inputs, consts),
                               core_ids=list(range(NCORES))).results
    keyname = "dbg_out" if upto != "x4" else "x4_out"
    out = np.zeros((B, L, D), np.float32)
    for k in range(NCORES):
        t0 = k * TOKB
        for b in range(2):
            out[b, t0:t0 + TOKB] = res[k][keyname][b * TOKB:(b + 1) * TOKB]
    return out


_BUILD_CACHE = {}


def _get_nc(name, builder):
    if name not in _BUILD_CACHE:
        _BUILD_CACHE[name] = builder()
    return _BUILD_CACHE[name]


def _assemble_bc(res, key):
    out = np.zeros((B, 32, L), dtype=res[0][key].dtype)
    for k in range(NCORES):
        b, t0 = k // 4, (k % 4) * TOK
        out[b, :, t0:t0 + TOK] = res[k][key]
    return out


def kernel(**inputs):
    return kernel_fused(inputs, upto="x4").astype(np.float32)


def kernel_unfused(**inputs):
    consts = make_consts()
    cores = list(range(NCORES))

    nc = _get_nc("A", build_launch_A)
    res = run_bass_kernel_spmd(nc, prep_launch_A(inputs, consts), core_ids=cores).results
    u0 = assemble_ch_major(res, "u_out")
    dt0 = assemble_ch_major(res, "dt_out")
    zs0 = assemble_ch_major(res, "zs_out")
    bc0 = _assemble_bc(res, "bc_out")

    nc = _get_nc("S1", lambda: build_launch_scan(ndir=1))
    dirs = [dict(dt=dt0, u=u0, Bm=bc0[:, 0:16], Cm=bc0[:, 16:32],
                 A_log=inputs["L0_attn_A_log"], D=inputs["L0_attn_D_skip"])]
    res = run_bass_kernel_spmd(nc, prep_launch_scan(dirs, zs0), core_ids=cores).results
    y0 = assemble_scan_y(res)

    nc = _get_nc("C", build_launch_C)
    res = run_bass_kernel_spmd(nc, prep_launch_C(inputs, consts, y0,
                                                 np.asarray(inputs["L0_attn_x"])),
                               core_ids=cores).results
    x2 = assemble_rows(res, "x2_out")

    nc = _get_nc("D", build_launch_D)
    res = run_bass_kernel_spmd(nc, prep_launch_D(inputs, consts, x2),
                               core_ids=cores).results
    uf = assemble_ch_major(res, "u_f_out")
    dtf = assemble_ch_major(res, "dt_f_out")
    ub = assemble_ch_major(res, "u_b_out")
    dtb = assemble_ch_major(res, "dt_b_out")
    zs1 = assemble_ch_major(res, "zs_out")
    bcf = _assemble_bc(res, "bc_f_out")
    bcb = _assemble_bc(res, "bc_b_out")

    nc = _get_nc("S2", lambda: build_launch_scan(ndir=2))
    dirs = [dict(dt=dtf, u=uf, Bm=bcf[:, 0:16], Cm=bcf[:, 16:32],
                 A_log=inputs["L1_attn_A_log"], D=inputs["L1_attn_D"]),
            dict(dt=dtb, u=ub, Bm=bcb[:, 0:16], Cm=bcb[:, 16:32],
                 A_log=inputs["L1_attn_A_b_log"], D=inputs["L1_attn_D_b"])]
    res = run_bass_kernel_spmd(nc, prep_launch_scan(dirs, zs1), core_ids=cores).results
    y1 = assemble_scan_y(res)

    nc = _get_nc("F", build_launch_F)
    res = run_bass_kernel_spmd(nc, prep_launch_F(inputs, consts, y1, x2),
                               core_ids=cores).results
    x4 = assemble_rows(res, "x4_out")
    return x4.astype(np.float32)
